# version 09
# baseline (speedup 1.0000x reference)
import contextlib
import numpy as np
import ml_dtypes
import concourse.bass as bass
import concourse.mybir as mybir
from concourse.bass_utils import run_bass_kernel_spmd

F32 = mybir.dt.float32
BF16 = mybir.dt.bfloat16
AF = mybir.ActivationFunctionType
ALU = mybir.AluOpType
AX = mybir.AxisListType.X

NCORE = 8
D = 1024
C8 = 8
DI = 2048
C16 = 16
DX = 3088
NEXP = 8
LORA_R = 64
TILE = 128
HALO = 8
TW = TILE + HALO
NEG = -1.0e30
ASTAGE = 99
DBG_TAPS = ()
EPS_LN = 1e-5
EPS_BIT = 1e-5


class Sem:
    def __init__(self, nc, es, name):
        self.h = es.enter_context(nc.semaphore(name))
        self.n = 0


_SEQ_OPS = frozenset((
    "wait_ge", "sem_inc", "nop", "engine_nop", "maybe_nop_then_inc",
    "maybe_drain_then_inc", "dma_start", "dma_start_transpose",
    "collective_compute", "inc_swdge_sem", "sem_clear", "reg_mov",
))


def _ap_names(args, kw):
    out = []
    for v in list(args) + list(kw.values()):
        t = getattr(v, "tensor", None)
        nm = getattr(t, "name", None)
        if nm is not None:
            out.append(nm)
    return out


class _EngRec:

    def __init__(self, eng, st, autodrain):
        self._eng = eng
        self._st = st
        self._ad = autodrain

    def __getattr__(self, name):
        a = getattr(self._eng, name)
        if not callable(a) or name in _SEQ_OPS or name.startswith("_"):
            return a
        def call(*args, **kw):
            st = self._st
            if self._ad:
                names = _ap_names(args, kw)
                wr = st.setdefault("written", set())
                if any(n in wr for n in names):
                    self._eng.drain()
                    wr.clear()
                if names:
                    wr.add(names[0])
                ao = kw.get("accum_out")
                t = getattr(ao, "tensor", None)
                if t is not None:
                    wr.add(t.name)
            r = a(*args, **kw)
            if r is not None and hasattr(r, "then_inc"):
                st["last"] = r
                st["armed"] = True
            return r
        return call


class KB:
    E = ("t", "v", "s", "g", "y")

    def __init__(self, nc, blk, es):
        self.nc, self.blk, self.es = nc, blk, es
        self.bufs = {e: [] for e in self.E}
        self._sid = 0
        self._pool = {}
        self.st = {e: {"last": None, "armed": False, "guard": None} for e in self.E}

    def sem(self, name=None):
        if name is not None and name in self._pool:
            return self._pool[name]
        self._sid += 1
        s = Sem(self.nc, self.es, f"{name or 'sm'}_{self._sid}")
        if name is not None:
            self._pool[name] = s
        return s

    def em(self, eng, fn):
        self.bufs[eng].append(fn)

    def wait(self, eng, sem, val=None):
        v = sem.n if val is None else val
        if v > 0:
            self.em(eng, lambda e, h=sem.h, v=v: e.wait_ge(h, v))

    def inc(self, eng, sem, k=1):
        st = self.st[eng]
        h = sem.h
        va = sem.n + k
        def go(e, st=st, h=h, k=k, va=va):
            if st["armed"]:
                st["last"].then_inc(h, k)
                st["armed"] = False
                st["guard"] = (h, va)
            elif st["guard"] is not None:
                gh, gv = st["guard"]
                e.wait_ge(gh, gv)
                e.sem_inc(h, k)
                st["guard"] = (h, va)
            else:
                e.sem_inc(h, k)
                st["guard"] = (h, va)
        self.em(eng, go)
        sem.n += k

    def flush(self):
        m = {"t": self.blk.tensor, "v": self.blk.vector, "s": self.blk.scalar,
             "g": self.blk.gpsimd, "y": self.blk.sync}
        for e in self.E:
            fns = self.bufs[e]
            if fns:
                self.bufs[e] = []
                st = self.st[e]
                ad = e in ("v", "s", "g")
                m[e](lambda eng, fns=fns, st=st, ad=ad:
                     [f(_EngRec(eng, st, ad)) for f in fns])

    def barrier(self, name="bar"):
        s = self.sem("_barrier")
        for e in self.E:
            self.inc(e, s)
        for e in self.E:
            self.wait(e, s)
        self.flush()


def slc(width, step=512):
    out, o = [], 0
    while o < width:
        w = min(step, width - o)
        out.append((o, w))
        o += w
    return out


def row2col(ap2d):
    return ap2d.rearrange("a (p o) -> (a p) o", p=128)


def build_nc(S):
    NT = S // TILE
    TPC = NT // NCORE
    SPT = 4 if TPC % 4 == 0 else (2 if TPC % 2 == 0 else 1)
    NSP = TPC // SPT
    SW = SPT * TW
    SC = TPC * TILE
    NSP2 = SC // 512
    assert S % 1024 == 0 and NT % NCORE == 0 and SC % 512 == 0

    nc = bass.Bass("TRN2", target_bir_lowering=False, detect_race_conditions=False)
    gstack = contextlib.ExitStack()

    def din(name, shape, dt=F32):
        return nc.dram_tensor(name, list(shape), dt, kind="ExternalInput").ap()

    def scr(name, shape, dt):
        return nc.dram_tensor(name, list(shape), dt, kind="Internal").ap()

    xsh = din("xsh", (D, TPC * TW))
    w_in_T = din("w_in_T", (D, 2 * DI))
    w_x_T = din("w_x_T", (DI, DX))
    w_out_T = din("w_out_T", (DI, D))
    wr_T = din("wr_T", (D, D))
    wk_T = din("wk_T", (D, D))
    wv_T = din("wv_T", (D, D))
    wattn_T = din("wattn_T", (D, 3 * D))
    wproj_nat = din("wproj_nat", (D, D))
    wproj_T = din("wproj_T", (D, D))
    la_aT = din("la_aT", (LORA_R, D))
    la_b = din("la_b", (LORA_R, 3 * D))
    lp_aT = din("lp_aT", (LORA_R, D))
    lp_b = din("lp_b", (LORA_R, D))
    wmod0 = din("wmod0", (D, 6 * D))
    wmod1 = din("wmod1", (D, 6 * D))
    c01T = din("c01T", (2 * D, 1))
    gwi = din("gwi", (D, NEXP))
    w1e = din("w1e", (NEXP, D, DI))
    w2e = din("w2e", (NEXP, DI, D))
    b1c = din("b1c", (128, NEXP * C16))
    b2n = din("b2n", (NEXP, D))
    b2c = din("b2c", (128, NEXP * C8))
    vtab = din("vtab", (128, 120))
    wconv_c = din("wconv_c", (128, C16 * 4))
    id_bf = din("id_bf", (128, 128), BF16)
    id_f32 = din("id_f32", (128, 128))
    trimask = din("trimask", (128, 1024))
    rowidx8 = din("rowidx8", (NEXP, 128))
    haloflag = din("haloflag", (128, 1))
    out = nc.dram_tensor("out", [SC, D], F32, kind="ExternalOutput").ap()

    s_wqin = scr("s_wqin", (D, DI), BF16)
    s_wqx = scr("s_wqx", (DI, DI), BF16)
    s_wqout = scr("s_wqout", (DI, D), BF16)
    s_wr = scr("s_wr", (D, D), BF16)
    s_wk = scr("s_wk", (D, D), BF16)
    s_wv = scr("s_wv", (D, D), BF16)
    s_R = scr("s_R", (3 * D, D), BF16)
    s_wm = scr("s_wm", (D, 3 * D), BF16)
    s_mod = scr("s_mod", (2, 6 * D), F32)
    s_dbgq = scr("s_dbgq", (128, 8), F32)
    s_x2 = scr("s_x2", (D, SC), F32)
    s_x3 = scr("s_x3", (D, SC), F32)
    s_xa = scr("s_xa", (3 * D, SC), BF16)
    s_macc = scr("s_macc", (D, SC), F32)
    s_ot = scr("s_ot", (D, SC), BF16)
    cc_k_in = scr("cc_k_in", (D, SC), BF16)
    cc_k_out = scr("cc_k_out", (NCORE * D, SC), BF16)
    ccvi = scr("ccvi", (SC, D), BF16)
    ccvo = scr("ccvo", (S, D), BF16)
    cc_bn_in = scr("cc_bn_in", (2, 3 * D), F32)
    cc_bn_out = scr("cc_bn_out", (2, 3 * D), F32)

    VT = {}
    _c = 0
    for nm, n in [("b_in", C16), ("b_conv", C16), ("b_x", C16), ("b_out", C8),
                  ("mur", C8), ("muk", C8), ("gma", 24), ("gmb", 24)]:
        VT[nm] = _c
        _c += n
    NVT_IN = _c
    for nm, n in [("murm", C8), ("mukm", C8), ("scl", 4)]:
        VT[nm] = _c
        _c += n
    assert _c <= 144

    def sbp(name, shape, dt=F32):
        return gstack.enter_context(nc.sbuf_tensor(name, list(shape), dt))

    vecs = sbp("vecs", (128, 144))
    wcv = sbp("wcv", (128, C16 * 4))

    def wcv_col(c, k):
        j = c * 4 + k
        return wcv[:, j:j + 1]
    mods = sbp("mods", (128, 96))
    MD = {}
    for j, nm in enumerate(["sa0", "ca0", "ga0", "sb0", "cb0", "gb0",
                            "sa1", "ca1", "ga1", "sb1", "cb1", "gb1"]):
        MD[nm] = j * 8

    def vc(nm, j=0):
        c = VT[nm] + j
        return vecs[:, c:c + 1]

    def mc(nm, j=0):
        c = MD[nm] + j
        return mods[:, c:c + 1]

    b1sb = sbp("b1sb", (128, NEXP * C16))
    idbf = sbp("idbf", (128, 128), BF16)
    idf = sbp("idf", (128, 128))
    trim = sbp("trim", (128, 1024))
    hflag = sbp("hflag", (128, 1))
    ones_r = sbp("ones_r", (1, 128))
    ones_c = sbp("ones_c", (128, 1))

    with nc.Block() as blk:
        K = KB(nc, blk, gstack)

        s0 = K.sem("c_ld")
        def ldc(y):
            y.dma_start(out=vecs[:, 0:NVT_IN], in_=vtab[:, 0:NVT_IN]).then_inc(s0.h, 16)
            y.dma_start(out=wcv[:], in_=wconv_c[:, :]).then_inc(s0.h, 16)
            y.dma_start(out=b1sb[:], in_=b1c[:, :]).then_inc(s0.h, 16)
            y.dma_start(out=idbf[:], in_=id_bf[:, :]).then_inc(s0.h, 16)
            y.dma_start(out=idf[:], in_=id_f32[:, :]).then_inc(s0.h, 16)
            y.dma_start(out=trim[:], in_=trimask[:, :]).then_inc(s0.h, 16)
            y.dma_start(out=hflag[:], in_=haloflag[:, :]).then_inc(s0.h, 16)
        K.em("y", ldc)
        s0.n += 7 * 16
        K.wait("v", s0)
        def ic(v):
            v.memset(ones_r[:], 1.0)
            v.memset(ones_c[:], 1.0)
            v.tensor_scalar(vecs[:, VT["murm"]:VT["murm"] + C8],
                            vecs[:, VT["mur"]:VT["mur"] + C8], -1.0, 1.0, ALU.mult, ALU.add)
            v.tensor_scalar(vecs[:, VT["mukm"]:VT["mukm"] + C8],
                            vecs[:, VT["muk"]:VT["muk"] + C8], -1.0, 1.0, ALU.mult, ALU.add)
        K.em("v", ic)
        K.flush()

        with contextlib.ExitStack() as ps:
            cc = ps.enter_context(nc.sbuf_tensor("md_c", [128, 16], F32))
            sil = ps.enter_context(nc.sbuf_tensor("md_s", [128, 16], F32))
            wmb = [ps.enter_context(nc.sbuf_tensor(f"md_w{j}", [128, 2048], F32))
                   for j in range(2)]
            mrow = ps.enter_context(nc.sbuf_tensor("md_r", [1, 6 * D], F32))
            psr = ps.enter_context(nc.psum_tensor("md_p", [1, 2048], F32))
            sld = K.sem("md_ld")
            ssi = K.sem("md_si")
            smm = K.sem("md_mm")
            sev = K.sem("md_ev")
            sst = K.sem("md_st")
            def ldcc(y):
                for li in range(2):
                    for ch in range(C8):
                        y.dma_start(out=cc[:, li * 8 + ch:li * 8 + ch + 1],
                                    in_=c01T[li * D + ch * 128:li * D + (ch + 1) * 128, 0:1]) \
                         .then_inc(sld.h, 16)
            K.em("y", ldcc)
            sld.n += 16 * 16
            K.wait("s", sld)
            K.em("s", lambda s: s.activation(sil[:, :], cc[:, :], AF.Silu))
            K.inc("s", ssi)
            wld = [0]
            for li, wmap in enumerate([wmod0, wmod1]):
                for grp in range(3):
                    g0 = grp * 2048
                    for ch in range(C8):
                        par = wld[0] % 2
                        if wld[0] >= 2:
                            K.wait("y", smm, wld[0] - 1)
                        def ldw(y, wmap=wmap, ch=ch, g0=g0, par=par):
                            y.dma_start(out=wmb[par][:, :],
                                        in_=wmap[ch * 128:(ch + 1) * 128, g0:g0 + 2048]) \
                             .then_inc(sld.h, 16)
                        K.em("y", ldw)
                        sld.n += 16
                        wld[0] += 1
                        K.wait("t", sld)
                        K.wait("t", ssi)
                        if ch == 0:
                            K.wait("t", sev)
                        def mm(t, ch=ch, li=li, par=par):
                            for o in range(0, 2048, 512):
                                t.matmul(psr[0:1, o:o + 512], sil[:, li * 8 + ch:li * 8 + ch + 1],
                                         wmb[par][:, o:o + 512],
                                         start=(ch == 0), stop=(ch == C8 - 1))
                        K.em("t", mm)
                        K.inc("t", smm)
                    K.wait("v", smm)
                    def ev(v, g0=g0):
                        v.tensor_copy(mrow[0:1, g0:g0 + 2048], psr[0:1, :])
                    K.em("v", ev)
                    K.inc("v", sev)
                K.wait("g", sev)
                def st(g, li=li):
                    g.dma_start(out=s_mod[li:li + 1, :], in_=mrow[0:1, :]).then_inc(sst.h, 16)
                K.em("g", st)
                sst.n += 16
            K.wait("y", sst)
            def ld2(y):
                for li in range(2):
                    for k6 in range(6):
                        for ch in range(C8):
                            col = li * 48 + k6 * 8 + ch
                            y.dma_start(out=mods[:, col:col + 1],
                                        in_=row2col(s_mod[li:li + 1,
                                                          k6 * D + ch * 128:k6 * D + (ch + 1) * 128])) \
                             .then_inc(sld.h, 16)
            K.em("y", ld2)
            sld.n += 96 * 16
            K.wait("v", sld)
            def drv(v):
                for nm in ["ga0", "gb0", "ga1", "gb1"]:
                    b = MD[nm]
                    for ch in range(C8):
                        a = mods[:, b + ch:b + ch + 1]
                        v.tensor_scalar(cc[:, 0:1], a, 0.0, 2.0, ALU.is_ge, ALU.mult)
                        v.tensor_scalar(cc[:, 0:1], cc[:, 0:1], -1.0, None, ALU.add)
                        v.tensor_scalar(cc[:, 1:2], a, -1.0, None, ALU.mult)
                        v.tensor_max(cc[:, 1:2], cc[:, 1:2], a)
                        v.tensor_scalar(cc[:, 1:2], cc[:, 1:2], 0.25, None, ALU.max)
                        v.tensor_mul(a, cc[:, 0:1], cc[:, 1:2])
                for nm in ["ca0", "cb0", "ca1", "cb1"]:
                    b = MD[nm]
                    v.tensor_scalar(mods[:, b:b + 8], mods[:, b:b + 8], 1.0, None, ALU.add)
            K.em("v", drv)
            K.barrier("md_done")

        with contextlib.ExitStack() as ps:
            wbuf = [ps.enter_context(nc.sbuf_tensor(f"q_w{j}", [128, 4096], F32))
                    for j in range(2)]
            qbuf = [ps.enter_context(nc.sbuf_tensor(f"q_q{j}", [128, 2048], BF16))
                    for j in range(2)]
            tbuf = ps.enter_context(nc.sbuf_tensor("q_t", [128, 2048], F32))
            dmy = ps.enter_context(nc.sbuf_tensor("q_d", [128, 4096], F32))
            strip = ps.enter_context(nc.sbuf_tensor("q_s", [128, 16], F32))
            thrt = ps.enter_context(nc.sbuf_tensor("q_thr", [128, 2], F32))
            sc1 = ps.enter_context(nc.sbuf_tensor("q_s1", [1, 1], F32))
            psq = ps.enter_context(nc.psum_tensor("q_p", [128, 8], F32))
            sld = K.sem("q_ld")
            sab = K.sem("q_ab")
            smm = K.sem("q_mm")
            sth = K.sem("q_th")
            sqz = K.sem("q_qz")
            sst = K.sem("q_st")
            srdn = K.sem("q_rdn")
            for wi, (src, nch, fc, uc, dst, sidx) in enumerate([
                    (w_in_T, C8, 2 * DI, DI, s_wqin, 0),
                    (w_x_T, C16, DX, DI, s_wqx, 1),
                    (w_out_T, C16, D, D, s_wqout, 2)]):
                for ch in range(nch):
                    par = ch % 2
                    if ch >= 2:
                        K.wait("y", sab, sab.n - 1)
                    elif wi > 0:
                        K.wait("y", sqz)
                    def ld(y, src=src, ch=ch, fc=fc, par=par):
                        y.dma_start(out=wbuf[par][:, 0:fc],
                                    in_=src[ch * 128:(ch + 1) * 128, 0:fc]).then_inc(sld.h, 16)
                    K.em("y", ld)
                    sld.n += 16
                    K.wait("s", sld)
                    if wi > 0 and ch == 0:
                        K.wait("s", srdn, wi)
                    def ab(s, ch=ch, fc=fc, par=par):
                        s.activation(dmy[:, 0:fc], wbuf[par][:, 0:fc], AF.Abs,
                                     accum_out=strip[:, ch:ch + 1])
                    K.em("s", ab)
                    K.inc("s", sab)
                K.wait("v", sab)
                def rd(v, nch=nch):
                    v.reduce_sum(strip[:, 0:1], strip[:, 0:nch], axis=AX)
                K.em("v", rd)
                K.inc("v", srdn)
                srd = sab
                K.inc("v", srd)
                K.wait("t", srd)
                def mm(t):
                    t.matmul(psq[0:1, 0:1], strip[:, 0:1], ones_c[:], start=True, stop=True)
                K.em("t", mm)
                K.inc("t", smm)
                K.wait("v", smm)
                def cp1(v):
                    v.tensor_copy(sc1[:, :], psq[0:1, 0:1])
                K.em("v", cp1)
                K.inc("v", smm)
                K.wait("t", smm)
                def mm2(t):
                    t.matmul(psq[:, 1:2], ones_r[:], sc1[:, :], start=True, stop=True)
                K.em("t", mm2)
                K.inc("t", smm)
                K.wait("v", smm)
                nelem = float(nch * 128 * fc)
                def thr(v, nelem=nelem, sidx=sidx):
                    v.tensor_scalar(vc("scl", sidx), psq[:, 1:2], 1.0 / nelem, EPS_BIT,
                                    ALU.mult, ALU.max)
                    v.tensor_scalar(thrt[:, 0:1], vc("scl", sidx), 0.5, None, ALU.mult)
                    v.tensor_scalar(thrt[:, 1:2], thrt[:, 0:1], -1.0, None, ALU.mult)
                K.em("v", thr)
                K.inc("v", sth)
                for ch in range(nch):
                    par = ch % 2
                    if ch >= 2:
                        K.wait("y", sqz, sqz.n - 1)
                    else:
                        K.wait("y", sab)
                    def ld(y, src=src, ch=ch, uc=uc, par=par):
                        y.dma_start(out=wbuf[par][:, 0:uc],
                                    in_=src[ch * 128:(ch + 1) * 128, 0:uc]).then_inc(sld.h, 16)
                    K.em("y", ld)
                    sld.n += 16
                    K.wait("v", sld)
                    K.wait("v", sth)
                    if ch >= 2:
                        K.wait("v", sst, sst.n - 16)
                    def qz(v, uc=uc, par=par):
                        v.tensor_scalar(tbuf[:, 0:uc], wbuf[par][:, 0:uc], thrt[:, 1:2],
                                        None, ALU.is_lt)
                        v.scalar_tensor_tensor(qbuf[par][:, 0:uc], wbuf[par][:, 0:uc],
                                               thrt[:, 0:1], tbuf[:, 0:uc],
                                               ALU.is_gt, ALU.subtract)
                    K.em("v", qz)
                    K.inc("v", sqz)
                    K.wait("g", sqz)
                    def st(g, dst=dst, ch=ch, uc=uc, par=par):
                        g.dma_start(out=dst[ch * 128:(ch + 1) * 128, 0:uc],
                                    in_=qbuf[par][:, 0:uc]).then_inc(sst.h, 16)
                    K.em("g", st)
                    sst.n += 16
                if wi == 0:
                    K.wait("g", sqz)
                    def dbgq(g):
                        with nc.allow_non_contiguous_dma(reason="debug tap"):
                            g.dma_start(out=s_dbgq[:, 0:2], in_=thrt[:, :]).then_inc(sst.h, 16)
                            g.dma_start(out=s_dbgq[:, 2:3], in_=vecs[:, 136:137]).then_inc(sst.h, 16)
                            g.dma_start(out=s_dbgq[:, 3:4], in_=tbuf[:, 0:1]).then_inc(sst.h, 16)
                    K.em("g", dbgq)
                    sst.n += 48
            K.wait("g", sst)
            for (src, dst) in [(wr_T, s_wr), (wk_T, s_wk), (wv_T, s_wv)]:
                for ch in range(C8):
                    def c1(g, src=src, ch=ch):
                        g.dma_start(out=qbuf[0][:, 0:1024],
                                    in_=src[ch * 128:(ch + 1) * 128, :]).then_inc(sst.h, 16)
                    K.em("g", c1)
                    sst.n += 16
                    K.wait("g", sst)
                    def c2(g, dst=dst, ch=ch):
                        g.dma_start(out=dst[ch * 128:(ch + 1) * 128, :],
                                    in_=qbuf[0][:, 0:1024]).then_inc(sst.h, 16)
                    K.em("g", c2)
                    sst.n += 16
                    K.wait("g", sst)
            laT = ps.enter_context(nc.sbuf_tensor("wm_a", [LORA_R, 128], BF16))
            lbB = ps.enter_context(nc.sbuf_tensor("wm_b", [LORA_R, 3 * D], BF16))
            rev = ps.enter_context(nc.sbuf_tensor("wm_r", [128, 3 * D], BF16))
            psl = ps.enter_context(nc.psum_tensor("wm_p", [128, 512], F32))
            w1 = K.sem("wm_1"); w2 = K.sem("wm_2"); w3 = K.sem("wm_3"); w4 = K.sem("wm_4")
            w5 = K.sem("wm_5"); w6 = K.sem("wm_6")
            def ldb(g):
                g.dma_start(out=lbB[:, :], in_=la_b[:, :]).then_inc(w1.h, 16)
            K.em("g", ldb)
            w1.n += 16
            for mch in range(C8):
                def lda(g, mch=mch):
                    g.dma_start(out=laT[:, :], in_=la_aT[:, mch * 128:(mch + 1) * 128]) \
                     .then_inc(w1.h, 16)
                K.em("g", lda)
                w1.n += 16
                K.wait("t", w1)
                if mch > 0:
                    K.wait("t", w4, w4.n)
                for (o, w) in slc(3 * D):
                    K.wait("t", w3, w3.n)
                    def mm(t, o=o, w=w):
                        t.matmul(psl[:, 0:w], laT[:, :], lbB[:, o:o + w], start=True, stop=True)
                    K.em("t", mm)
                    K.inc("t", w2)
                    K.wait("v", w2)
                    def ev(v, o=o, w=w):
                        v.tensor_copy(rev[:, o:o + w], psl[:, 0:w])
                    K.em("v", ev)
                    K.inc("v", w3)
                K.wait("g", w3)
                def st(g, mch=mch):
                    g.dma_start(out=s_R[mch * 384:(mch + 1) * 384, :]
                                .rearrange("(p a) d -> p a d", p=128),
                                in_=rev[:, :].rearrange("p (a d) -> p a d", a=3)) \
                     .then_inc(w4.h, 16)
                K.em("g", st)
                w4.n += 16
            watf = dmy
            K.wait("y", w4)
            for ch in range(C8):
                if ch > 0:
                    K.wait("y", w6, w6.n)
                def ld(y, ch=ch):
                    y.dma_start_transpose(rev[:, :], s_R[:, ch * 128:(ch + 1) * 128]) \
                     .then_inc(w5.h, 16)
                    y.dma_start(out=watf[:, 0:3 * D],
                                in_=wattn_T[ch * 128:(ch + 1) * 128, :]).then_inc(w5.h, 16)
                K.em("y", ld)
                w5.n += 32
                K.wait("v", w5)
                def add(v, ch=ch):
                    v.tensor_add(rev[:, :], watf[:, 0:3 * D], rev[:, :])
                K.em("v", add)
                K.inc("v", w2)
                K.wait("g", w2)
                def st(g, ch=ch):
                    g.dma_start(out=s_wm[ch * 128:(ch + 1) * 128, :], in_=rev[:, :]) \
                     .then_inc(w6.h, 16)
                K.em("g", st)
                w6.n += 16
                K.wait("v", w6)
            K.barrier("ph0_done")

        _lnid = [0]

        def make_ln(ps):
            _lnid[0] += 1
            u = _lnid[0]
            h = {}
            h["px"] = ps.enter_context(nc.psum_tensor(f"ln{u}_px", [1, 512], F32))
            h["pq"] = ps.enter_context(nc.psum_tensor(f"ln{u}_pq", [1, 512], F32))
            h["pA"] = ps.enter_context(nc.psum_tensor(f"ln{u}_pA", [128, 512], F32))
            h["pB"] = ps.enter_context(nc.psum_tensor(f"ln{u}_pB", [128, 512], F32))
            h["sq"] = ps.enter_context(nc.sbuf_tensor(f"ln{u}_sq", [128, 512], F32))
            h["rw"] = ps.enter_context(nc.sbuf_tensor(f"ln{u}_rw", [1, 2048], F32))
            h["tm"] = ps.enter_context(nc.sbuf_tensor(f"ln{u}_tm", [128, 512], F32))
            h["ssq"] = K.sem("ln_ssq")
            h["smm"] = K.sem("ln_smm")
            h["srw"] = K.sem("ln_srw")
            h["sbc"] = K.sem("ln_sbc")
            h["sap"] = K.sem("ln_sap")
            return h

        def emit_ln(h, xs, W, hout, ca, sa, zout=None):
            for (o, w) in slc(W):
                for ch in range(C8):
                    xap, xo = xs[ch]
                    K.wait("s", h["smm"])
                    def sq(s, xap=xap, xo=xo, o=o, w=w):
                        s.activation(h["sq"][:, 0:w], xap[:, xo + o:xo + o + w], AF.Square)
                    K.em("s", sq)
                    K.inc("s", h["ssq"])
                    K.wait("t", h["ssq"])
                    if ch == 0:
                        K.wait("t", h["srw"])
                    def mm(t, xap=xap, xo=xo, o=o, w=w, ch=ch):
                        t.matmul(h["px"][0:1, 0:w], ones_c[:], xap[:, xo + o:xo + o + w],
                                 start=(ch == 0), stop=(ch == C8 - 1))
                        t.matmul(h["pq"][0:1, 0:w], ones_c[:], h["sq"][:, 0:w],
                                 start=(ch == 0), stop=(ch == C8 - 1))
                    K.em("t", mm)
                    K.inc("t", h["smm"])
                K.wait("v", h["smm"])
                def rows(v, w=w):
                    m = h["rw"][0:1, 0:w]
                    e2 = h["rw"][0:1, 512:512 + w]
                    A = h["rw"][0:1, 1024:1024 + w]
                    B = h["rw"][0:1, 1536:1536 + w]
                    v.tensor_scalar(m, h["px"][0:1, 0:w], 1.0 / D, None, ALU.mult)
                    v.tensor_scalar(e2, h["pq"][0:1, 0:w], 1.0 / D, None, ALU.mult)
                    v.tensor_mul(B, m, m)
                    v.tensor_sub(e2, e2, B)
                    v.tensor_scalar(e2, e2, EPS_LN, None, ALU.add)
                K.em("v", rows)
                K.inc("v", h["srw"])
                K.wait("s", h["srw"])
                def rsq(s, w=w):
                    s.activation(h["rw"][0:1, 1024:1024 + w], h["rw"][0:1, 512:512 + w], AF.Sqrt)
                K.em("s", rsq)
                K.inc("s", h["srw"])
                K.wait("v", h["srw"])
                def rows2(v, w=w):
                    m = h["rw"][0:1, 0:w]
                    A = h["rw"][0:1, 1024:1024 + w]
                    B = h["rw"][0:1, 1536:1536 + w]
                    v.reciprocal(A, A)
                    v.tensor_mul(B, m, A)
                    v.tensor_scalar(B, B, -1.0, None, ALU.mult)
                K.em("v", rows2)
                K.inc("v", h["srw"])
                K.wait("t", h["srw"])
                K.wait("t", h["sap"])
                def bc(t, w=w):
                    t.matmul(h["pA"][:, 0:w], ones_r[:], h["rw"][0:1, 1024:1024 + w],
                             start=True, stop=True)
                    t.matmul(h["pB"][:, 0:w], ones_r[:], h["rw"][0:1, 1536:1536 + w],
                             start=True, stop=True)
                K.em("t", bc)
                K.inc("t", h["sbc"])
                K.wait("v", h["sbc"])
                for ch in range(C8):
                    xap, xo = xs[ch]
                    hap, ho = hout[ch]
                    zap = zout[ch] if zout else None
                    def ap_(v, xap=xap, xo=xo, hap=hap, ho=ho, zap=zap, o=o, w=w, ch=ch):
                        tgt = zap[0][:, zap[1] + o:zap[1] + o + w] if zap else h["tm"][:, 0:w]
                        v.tensor_mul(tgt, xap[:, xo + o:xo + o + w], h["pA"][:, 0:w])
                        v.tensor_add(tgt, tgt, h["pB"][:, 0:w])
                        v.tensor_scalar(hap[:, ho + o:ho + o + w], tgt,
                                        mc(ca, ch), mc(sa, ch), ALU.mult, ALU.add)
                    K.em("v", ap_)
                K.inc("v", h["sap"])
                K.wait("t", h["sap"])

        _mmid = [0]

        class MM:
            def __init__(self, ps):
                _mmid[0] += 1
                u = _mmid[0]
                self.pb = [ps.enter_context(nc.psum_tensor(f"mm{u}p{j}", [128, 512], F32))
                           for j in range(2)]
                self.sm = [K.sem(f"mm_s{j}") for j in range(2)]
                self.se = [K.sem(f"mm_e{j}") for j in range(2)]
                self.uses = [0, 0]
                self.rr = 0

            def stage(self, n_out, W, nin, lhsT_fn, rhs_fn, epil, ep_eng="v",
                      extra_reader=None):
                for o in range(n_out):
                    for (off, w) in slc(W):
                        j = self.rr % 2
                        self.rr += 1
                        K.wait("t", self.se[j])
                        self.uses[j] += 1
                        psu = self.pb[j]
                        def mm(t, o=o, off=off, w=w, psu=psu, nin=nin):
                            for c in range(nin):
                                t.matmul(psu[:, 0:w], lhsT_fn(c, o), rhs_fn(c, off, w),
                                         start=(c == 0), stop=(c == nin - 1))
                        K.em("t", mm)
                        K.inc("t", self.sm[j])
                        K.wait(ep_eng, self.sm[j])
                        def ep(e, o=o, off=off, w=w, psu=psu):
                            epil(e, o, off, w, psu)
                        K.em(ep_eng, ep)
                        K.inc(ep_eng, self.se[j])
                        if extra_reader:
                            eng2, ep2 = extra_reader
                            K.wait(eng2, self.sm[j])
                            def e2(e, o=o, off=off, w=w, psu=psu):
                                ep2(e, o, off, w, psu)
                            K.em(eng2, e2)
                            K.inc(eng2, self.se[j])

            def drain(self, eng):
                for j in range(2):
                    K.wait(eng, self.se[j])

        with contextlib.ExitStack() as ps:
            xsp = [ps.enter_context(nc.sbuf_tensor(f"xsp{c}", [128, SW], F32))
                   for c in range(C8)]
            hb = [ps.enter_context(nc.sbuf_tensor(f"hb{c}", [128, SW], BF16))
                  for c in range(C8)]
            xib = [ps.enter_context(nc.sbuf_tensor(f"xib{c}", [128, SW], BF16))
                   for c in range(C16)]
            xcb = [ps.enter_context(nc.sbuf_tensor(f"xcb{c}", [128, SW], BF16))
                   for c in range(C16)]
            xrb = [ps.enter_context(nc.sbuf_tensor(f"xrb{c}", [128, SW], BF16))
                   for c in range(C8)]
            rb = [ps.enter_context(nc.sbuf_tensor(f"rb{c}", [128, SW], BF16))
                  for c in range(C8)]
            kab = [ps.enter_context(nc.sbuf_tensor(f"kab{c}", [128, SW], BF16))
                   for c in range(C8)]
            wp = [ps.enter_context(nc.sbuf_tensor(f"wp{c}", [128, 2048], BF16))
                  for c in range(C16)]
            ca1 = ps.enter_context(nc.sbuf_tensor("cva", [128, SW], F32))
            ca2 = ps.enter_context(nc.sbuf_tensor("cvb", [128, SW], F32))
            ca3 = ps.enter_context(nc.sbuf_tensor("cvc", [128, SW], F32))
            ept = ps.enter_context(nc.sbuf_tensor("ept", [128, 512], F32))
            lnh = make_ln(ps)
            mm = MM(ps)
            s_x = K.sem("p1x")
            s_w = K.sem("p1w")
            s_wdone = K.sem("p1wd")
            s_cv = K.sem("p1cv")
            s_cg = K.sem("p1cg")
            s_sil = K.sem("p1sil")
            s_y = K.sem("p1y")
            s_x2st = K.sem("p1x2")
            s_sp = K.sem("p1sp")

            wstage = [0]

            def loadw(src, nch, cols):
                if wstage[0] > 0:
                    K.wait("y", s_wdone, wstage[0])
                def ld(y, src=src, nch=nch, cols=cols):
                    for c in range(nch):
                        y.dma_start(out=wp[c][:, 0:cols],
                                    in_=src[c * 128:(c + 1) * 128, 0:cols]).then_inc(s_w.h, 16)
                K.em("y", ld)
                s_w.n += 16 * nch
                wstage[0] += 1
                K.wait("t", s_w)

            def z0(v):
                for c in range(C16):
                    v.memset(xcb[c][:, 0:4], 0.0)
            K.em("v", z0)
            s_z0 = K.sem("p1z0")
            K.inc("v", s_z0)
            K.wait("t", s_z0)

            for span in range(NSP):
                sb0 = span * SW
                if span > 0:
                    K.wait("y", s_x2st, span * 16 * C8)
                def ldx(y, sb0=sb0):
                    for c in range(C8):
                        y.dma_start(out=xsp[c][:, :],
                                    in_=xsh[c * 128:(c + 1) * 128, sb0:sb0 + SW]) \
                         .then_inc(s_x.h, 16)
                K.em("y", ldx)
                s_x.n += 16 * C8
                for e in ("v", "s", "t"):
                    K.wait(e, s_x)
                emit_ln(lnh, [(xsp[c], 0) for c in range(C8)], SW,
                        [(hb[c], 0) for c in range(C8)], "ca0", "sa0")
                loadw(s_wqin, C8, 2048)
                def ep_u(v, o, off, w, psu):
                    v.tensor_scalar(xib[o][:, off:off + w], psu[:, 0:w],
                                    vc("scl", 0), vc("b_in", o), ALU.mult, ALU.add)
                mm.stage(C16, SW, C8,
                         lambda c, o: wp[c][:, o * 128:(o + 1) * 128],
                         lambda c, off, w: hb[c][:, off:off + w], ep_u)
                K.inc("t", s_wdone)
                if span == 0:
                    mm.drain("v")
                    def hz(v):
                        for c in range(C16):
                            v.tensor_scalar(xib[c][:, 4:8], xib[c][:, 4:8],
                                            hflag[:, 0:1], None, ALU.mult)
                    K.em("v", hz)
                for c in range(C16):
                    mm.drain("v")
                    if c > 0 or span > 0:
                        K.wait("v", s_sil, s_sil.n)
                    def cva(v, c=c):
                        v.tensor_scalar(ca1[:, 3:SW], xib[c][:, 0:SW - 3],
                                        wcv_col(c, 0), None, ALU.mult)
                        v.scalar_tensor_tensor(ca1[:, 3:SW], xib[c][:, 1:SW - 2],
                                               wcv_col(c, 1), ca1[:, 3:SW], ALU.mult, ALU.add)
                        v.scalar_tensor_tensor(ca1[:, 3:SW], xib[c][:, 2:SW - 1],
                                               wcv_col(c, 2), ca1[:, 3:SW], ALU.mult, ALU.add)
                        v.scalar_tensor_tensor(ca1[:, 3:SW], xib[c][:, 3:SW],
                                               wcv_col(c, 3), ca1[:, 3:SW], ALU.mult, ALU.add)
                    K.em("v", cva)
                    K.inc("v", s_cv)
                    K.wait("s", s_cv)
                    def sil(s, c=c):
                        s.activation(xcb[c][:, 3:SW], ca1[:, 3:SW], AF.Silu,
                                     bias=vc("b_conv", c))
                    K.em("s", sil)
                    K.inc("s", s_sil)
                loadw(s_wqx, C16, 2048)
                K.wait("t", s_sil)
                def gbuf(o):
                    return xrb[o] if o < C8 else rb[o - C8]
                def ep_p(s, o, off, w, psu):
                    s.activation(gbuf(o)[:, off:off + w], psu[:, 0:w], AF.Sigmoid,
                                 bias=vc("b_x", o), scale=vc("scl", 1))
                mm.stage(C16, SW, C16,
                         lambda c, o: wp[c][:, o * 128:(o + 1) * 128],
                         lambda c, off, w: xcb[c][:, off:off + w], ep_p, ep_eng="s")
                K.wait("v", mm.sm[0], mm.sm[0].n)
                K.wait("v", mm.sm[1], mm.sm[1].n)
                mm.drain("v")
                def ymul(v):
                    for c in range(C16):
                        v.tensor_mul(xcb[c][:, :], xcb[c][:, :], gbuf(c)[:, :])
                K.em("v", ymul)
                K.inc("v", s_y)
                K.wait("t", s_y)
                K.inc("t", s_wdone)
                loadw(s_wqout, C16, 1024)
                def ep_o(v, o, off, w, psu):
                    v.tensor_scalar(ept[:, 0:w], psu[:, 0:w], vc("scl", 2), vc("b_out", o),
                                    ALU.mult, ALU.add)
                    v.scalar_tensor_tensor(xsp[o][:, off:off + w], ept[:, 0:w], mc("ga0", o),
                                           xsp[o][:, off:off + w], ALU.mult, ALU.add)
                mm.stage(C8, SW, C16,
                         lambda c, o: wp[c][:, o * 128:(o + 1) * 128],
                         lambda c, off, w: xcb[c][:, off:off + w], ep_o)
                K.inc("t", s_wdone)
                mm.drain("t")
                mm.drain("s")
                emit_ln(lnh, [(xsp[c], 0) for c in range(C8)], SW,
                        [(hb[c], 0) for c in range(C8)], "cb0", "sb0")
                if span == 0:
                    def hz2(v):
                        for c in range(C8):
                            v.tensor_scalar(hb[c][:, 7:8], hb[c][:, 7:8],
                                            hflag[:, 0:1], None, ALU.mult)
                    K.em("v", hz2)
                def shifts(v, mu, mum):
                    for c in range(C8):
                        v.tensor_scalar(ca1[:, 8:SW], hb[c][:, 8:SW], vc(mu, c),
                                        None, ALU.mult)
                        v.scalar_tensor_tensor(xrb[c][:, 8:SW], hb[c][:, 7:SW - 1],
                                               vc(mum, c), ca1[:, 8:SW], ALU.mult, ALU.add)
                K.em("v", lambda v: shifts(v, "mur", "murm"))
                K.inc("v", s_cv)
                loadw(s_wr, C8, 1024)
                K.wait("t", s_cv)
                def ep_r(s, o, off, w, psu):
                    s.activation(rb[o][:, 8 + off:8 + off + w], psu[:, 0:w], AF.Sigmoid)
                mm.stage(C8, SW - 8, C8,
                         lambda c, o: wp[c][:, o * 128:(o + 1) * 128],
                         lambda c, off, w: xrb[c][:, 8 + off:8 + off + w], ep_r, ep_eng="s")
                K.inc("t", s_wdone)
                mm.drain("v")
                mm.drain("t")
                K.em("v", lambda v: shifts(v, "muk", "mukm"))
                K.inc("v", s_cv)
                loadw(s_wk, C8, 1024)
                K.wait("t", s_cv)
                def ep_k(s, o, off, w, psu):
                    s.activation(kab[o][:, 8 + off:8 + off + w], psu[:, 0:w], AF.Relu)
                mm.stage(C8, SW - 8, C8,
                         lambda c, o: wp[c][:, o * 128:(o + 1) * 128],
                         lambda c, off, w: xrb[c][:, 8 + off:8 + off + w], ep_k, ep_eng="s")
                K.inc("t", s_wdone)
                mm.drain("v")
                sv_k = s_cv
                K.inc("s", sv_k)
                K.wait("v", sv_k)
                def ksq(v):
                    for c in range(C8):
                        v.tensor_mul(kab[c][:, 8:SW], kab[c][:, 8:SW], kab[c][:, 8:SW])
                K.em("v", ksq)
                K.inc("v", s_cv)
                loadw(s_wv, C8, 1024)
                K.wait("t", s_cv)
                def ep_v(v, o, off, w, psu):
                    v.tensor_mul(ept[:, 0:w], rb[o][:, 8 + off:8 + off + w], psu[:, 0:w])
                    v.scalar_tensor_tensor(xsp[o][:, 8 + off:8 + off + w], ept[:, 0:w],
                                           mc("gb0", o), xsp[o][:, 8 + off:8 + off + w],
                                           ALU.mult, ALU.add)
                mm.stage(C8, SW - 8, C8,
                         lambda c, o: wp[c][:, o * 128:(o + 1) * 128],
                         lambda c, off, w: kab[c][:, 8 + off:8 + off + w], ep_v)
                K.inc("t", s_wdone)
                mm.drain("y")
                K.inc("v", s_cv)
                K.wait("y", s_cv)
                def stx2(y, span=span):
                    for c in range(C8):
                        src = xsp[c][:, :].rearrange("p (a w) -> p a w", a=SPT)[:, :, 8:TW]
                        y.dma_start(out=s_x2[c * 128:(c + 1) * 128,
                                             span * SPT * 128:(span + 1) * SPT * 128]
                                    .rearrange("p (a w) -> p a w", a=SPT),
                                    in_=src).then_inc(s_x2st.h, 16)
                K.em("y", stx2)
                s_x2st.n += 16 * C8
                for e in ("t", "v", "s", "g"):
                    K.inc(e, s_sp)
                for e in ("t", "v", "s", "g"):
                    K.wait(e, s_sp)
                K.flush()
            K.barrier("ph1_done")

        QOFF = 24 * NSP2
        bnS = sbp("bnS", (128, 2 * QOFF))
        bnA = sbp("bnA", (128, 24))
        bnB = sbp("bnB", (128, 24))
        with contextlib.ExitStack() as ps:
            x2sp = [ps.enter_context(nc.sbuf_tensor(f"x2sp{c}", [128, 512], F32))
                    for c in range(C8)]
            h3sp = [ps.enter_context(nc.sbuf_tensor(f"h3sp{c}", [128, 512], BF16))
                    for c in range(C8)]
            wmP = [ps.enter_context(nc.sbuf_tensor(f"wmP{c}", [128, 3 * D], BF16))
                   for c in range(C8)]
            xasp = ps.enter_context(nc.sbuf_tensor("xasp", [128, 512], BF16))
            dum2 = ps.enter_context(nc.sbuf_tensor("dum2", [128, 512], F32))
            lnh = make_ln(ps)
            mm = MM(ps)
            s_x = K.sem("p2x")
            s_wl = K.sem("p2w")
            s_st = K.sem("p2st")
            def ldwm(y):
                for c in range(C8):
                    y.dma_start(out=wmP[c][:, :], in_=s_wm[c * 128:(c + 1) * 128, :]) \
                     .then_inc(s_wl.h, 16)
            K.em("y", ldwm)
            s_wl.n += 16 * C8
            K.wait("t", s_wl)
            for sp in range(NSP2):
                c0 = sp * 512
                if sp > 0:
                    K.wait("y", lnh["sap"])
                    K.wait("y", lnh["ssq"])
                    K.wait("y", lnh["smm"])
                def ldx(y, c0=c0):
                    for c in range(C8):
                        y.dma_start(out=x2sp[c][:, :],
                                    in_=s_x2[c * 128:(c + 1) * 128, c0:c0 + 512]) \
                         .then_inc(s_x.h, 16)
                K.em("y", ldx)
                s_x.n += 16 * C8
                for e in ("v", "s", "t"):
                    K.wait(e, s_x)
                emit_ln(lnh, [(x2sp[c], 0) for c in range(C8)], 512,
                        [(h3sp[c], 0) for c in range(C8)], "ca1", "sa1")
                for o in range(24):
                    j = mm.rr % 2
                    mm.rr += 1
                    K.wait("t", mm.se[j])
                    mm.uses[j] += 1
                    psu = mm.pb[j]
                    def qmm(t, o=o, psu=psu):
                        for c in range(C8):
                            t.matmul(psu[:, :], wmP[c][:, o * 128:(o + 1) * 128],
                                     h3sp[c][:, :], start=(c == 0), stop=(c == C8 - 1))
                    K.em("t", qmm)
                    K.inc("t", mm.sm[j])
                    K.wait("s", mm.sm[j])
                    if s_st.n > 0:
                        K.wait("s", s_st, s_st.n)
                    def qep(s, o=o, sp=sp, psu=psu):
                        s.activation(xasp[:, :], psu[:, :], AF.Copy,
                                     accum_out=bnS[:, o * NSP2 + sp:o * NSP2 + sp + 1])
                        s.activation(dum2[:, :], psu[:, :], AF.Square,
                                     accum_out=bnS[:, QOFF + o * NSP2 + sp:QOFF + o * NSP2 + sp + 1])
                    K.em("s", qep)
                    K.inc("s", mm.se[j])
                    K.wait("y", mm.se[j])
                    def qst(y, o=o, c0=c0):
                        y.dma_start(out=s_xa[o * 128:(o + 1) * 128, c0:c0 + 512],
                                    in_=xasp[:, :]).then_inc(s_st.h, 16)
                    K.em("y", qst)
                    s_st.n += 16
                K.flush()
            K.barrier("ph2_done")

        with contextlib.ExitStack() as ps:
            dumb = ps.enter_context(nc.sbuf_tensor("bn_d", [128, 24], F32))
            sr = K.sem("bn_r"); st_ = K.sem("bn_st"); scc = K.sem("bn_cc")
            sl2 = K.sem("bn_ld"); sab = K.sem("bn_ab")
            def red(v):
                for o in range(24):
                    if NSP2 == 1:
                        v.tensor_copy(bnA[:, o:o + 1], bnS[:, o:o + 1])
                        v.tensor_copy(bnB[:, o:o + 1], bnS[:, QOFF + o:QOFF + o + 1])
                    else:
                        v.reduce_sum(bnA[:, o:o + 1], bnS[:, o * NSP2:(o + 1) * NSP2], axis=AX)
                        v.reduce_sum(bnB[:, o:o + 1],
                                     bnS[:, QOFF + o * NSP2:QOFF + (o + 1) * NSP2], axis=AX)
            K.em("v", red)
            K.inc("v", sr)
            K.wait("g", sr)
            def bst(g):
                for o in range(24):
                    g.dma_start(out=row2col(cc_bn_in[0:1, o * 128:(o + 1) * 128]),
                                in_=bnA[:, o:o + 1]).then_inc(st_.h, 16)
                    g.dma_start(out=row2col(cc_bn_in[1:2, o * 128:(o + 1) * 128]),
                                in_=bnB[:, o:o + 1]).then_inc(st_.h, 16)
            K.em("g", bst)
            st_.n += 48 * 16
            K.wait("g", st_)
            def bcc(g):
                g.collective_compute("AllReduce", ALU.add,
                                     replica_groups=[list(range(NCORE))],
                                     ins=[cc_bn_in.opt()], outs=[cc_bn_out.opt()]) \
                 .then_inc(scc.h, 1)
                g.wait_ge(scc.h, 1)
            K.em("g", bcc)
            scc.n += 1
            K.wait("y", scc)
            def bld(y):
                for o in range(24):
                    y.dma_start(out=bnA[:, o:o + 1],
                                in_=row2col(cc_bn_out[0:1, o * 128:(o + 1) * 128])) \
                     .then_inc(sl2.h, 16)
                    y.dma_start(out=bnB[:, o:o + 1],
                                in_=row2col(cc_bn_out[1:2, o * 128:(o + 1) * 128])) \
                     .then_inc(sl2.h, 16)
            K.em("y", bld)
            sl2.n += 48 * 16
            K.wait("v", sl2)
            def ab1(v):
                v.tensor_scalar(bnA[:, :], bnA[:, :], 1.0 / S, None, ALU.mult)
                v.tensor_scalar(bnB[:, :], bnB[:, :], 1.0 / S, None, ALU.mult)
                v.tensor_mul(dumb[:, :], bnA[:, :], bnA[:, :])
                v.tensor_sub(bnB[:, :], bnB[:, :], dumb[:, :])
                v.tensor_scalar(bnB[:, :], bnB[:, :], EPS_LN, None, ALU.add)
            K.em("v", ab1)
            K.inc("v", sab)
            K.wait("s", sab)
            K.em("s", lambda s: s.activation(bnB[:, :], bnB[:, :], AF.Sqrt))
            K.inc("s", sab)
            K.wait("v", sab)
            def ab2(v):
                v.reciprocal(bnB[:, :], bnB[:, :])
                v.tensor_mul(bnB[:, :], bnB[:, :], vecs[:, VT["gma"]:VT["gma"] + 24])
                v.tensor_scalar(bnB[:, 0:8], bnB[:, 0:8], 1.0 / 32.0, None, ALU.mult)
                v.tensor_mul(bnA[:, :], bnA[:, :], bnB[:, :])
                v.tensor_copy(dumb[:, :], vecs[:, VT["gmb"]:VT["gmb"] + 24])
                v.tensor_scalar(dumb[:, 0:8], dumb[:, 0:8], 1.0 / 32.0, None, ALU.mult)
                v.tensor_sub(bnA[:, :], dumb[:, :], bnA[:, :])
            K.em("v", ab2)
            K.barrier("bn_done")

        ps34 = contextlib.ExitStack()
        qres = [ps34.enter_context(nc.sbuf_tensor(f"qres{c}", [128, SC], BF16))
                for c in range(C8)]
        with contextlib.ExitStack() as ps:
            xb = ps.enter_context(nc.sbuf_tensor("n_x", [128, 512], BF16))
            kv = ps.enter_context(nc.sbuf_tensor("n_kv", [128, 512], BF16))
            vt = ps.enter_context(nc.sbuf_tensor("n_vt", [128, 1024], BF16))
            ptr = ps.enter_context(nc.psum_tensor("n_tr", [128, 1024], BF16))
            s_x = K.sem("p3x"); s_n = K.sem("p3n"); s_t = K.sem("p3t")
            s_c = K.sem("p3c"); s_st = K.sem("p3st"); s_cc = K.sem("p3cc")
            for sp in range(NSP2):
                c0 = sp * 512
                for c in range(C8):
                    def ldq(y, c=c, c0=c0):
                        y.dma_start(out=xb[:, :], in_=s_xa[c * 128:(c + 1) * 128, c0:c0 + 512]) \
                         .then_inc(s_x.h, 16)
                    K.em("y", ldq)
                    s_x.n += 16
                    K.wait("v", s_x)
                    def nq(v, c=c, c0=c0):
                        v.tensor_scalar(qres[c][:, c0:c0 + 512], xb[:, :],
                                        bnB[:, c:c + 1], bnA[:, c:c + 1], ALU.mult, ALU.add)
                    K.em("v", nq)
                    K.inc("v", s_n)
                    K.wait("y", s_n)
                for c in range(C8):
                    def ldk(y, c=c, c0=c0):
                        y.dma_start(out=xb[:, :],
                                    in_=s_xa[(8 + c) * 128:(9 + c) * 128, c0:c0 + 512]) \
                         .then_inc(s_x.h, 16)
                    K.em("y", ldk)
                    s_x.n += 16
                    K.wait("v", s_x)
                    if s_st.n:
                        K.wait("v", s_st, s_st.n)
                    def nk(v, c=c):
                        v.tensor_scalar(kv[:, :], xb[:, :],
                                        bnB[:, 8 + c:9 + c], bnA[:, 8 + c:9 + c],
                                        ALU.mult, ALU.add)
                    K.em("v", nk)
                    K.inc("v", s_n)
                    K.wait("g", s_n)
                    def stk(g, c=c, c0=c0):
                        g.dma_start(out=cc_k_in[c * 128:(c + 1) * 128, c0:c0 + 512],
                                    in_=kv[:, :]).then_inc(s_st.h, 16)
                    K.em("g", stk)
                    s_st.n += 16
                    K.wait("y", s_st)
                for tt in range(4):
                    for c in range(C8):
                        def ldv(y, c=c, c0=c0, tt=tt):
                            y.dma_start(out=xb[:, 0:128],
                                        in_=s_xa[(16 + c) * 128:(17 + c) * 128,
                                                 c0 + tt * 128:c0 + (tt + 1) * 128]) \
                             .then_inc(s_x.h, 16)
                        K.em("y", ldv)
                        s_x.n += 16
                        K.wait("v", s_x)
                        if s_st.n:
                            K.wait("v", s_st, s_st.n)
                        def nv(v, c=c):
                            v.tensor_scalar(kv[:, 0:128], xb[:, 0:128],
                                            bnB[:, 16 + c:17 + c], bnA[:, 16 + c:17 + c],
                                            ALU.mult, ALU.add)
                        K.em("v", nv)
                        K.inc("v", s_n)
                        K.wait("t", s_n)
                        K.wait("t", s_c, s_c.n)
                        def tr(t, c=c):
                            t.transpose(ptr[:, c * 128:(c + 1) * 128], kv[:, 0:128], idbf[:])
                        K.em("t", tr)
                        K.inc("t", s_t)
                        K.wait("v", s_t)
                        def cp(v, c=c):
                            v.tensor_copy(vt[:, c * 128:(c + 1) * 128],
                                          ptr[:, c * 128:(c + 1) * 128])
                        K.em("v", cp)
                        K.inc("v", s_c)
                        K.wait("y", s_c)
                    K.wait("g", s_c)
                    def stv(g, c0=c0, tt=tt):
                        g.dma_start(out=ccvi[c0 + tt * 128:c0 + (tt + 1) * 128, :],
                                    in_=vt[:, :]).then_inc(s_st.h, 16)
                    K.em("g", stv)
                    s_st.n += 16
                    K.wait("v", s_st)
                K.flush()
            K.wait("g", s_st)
            def gath(g):
                g.collective_compute("AllGather", ALU.bypass,
                                     replica_groups=[list(range(NCORE))],
                                     ins=[cc_k_in.opt()], outs=[cc_k_out.opt()]) \
                 .then_inc(s_cc.h, 1)
                g.wait_ge(s_cc.h, 1)
                g.collective_compute("AllGather", ALU.bypass,
                                     replica_groups=[list(range(NCORE))],
                                     ins=[ccvi.opt()], outs=[ccvo.opt()]) \
                 .then_inc(s_cc.h, 1)
                g.wait_ge(s_cc.h, 2)
            K.em("g", gath)
            s_cc.n += 2
            K.barrier("gather_done")

        with contextlib.ExitStack() as ps:
            otbuf = ps.enter_context(nc.sbuf_tensor("otbuf", [128, 1024], BF16))
            kb_ = [ps.enter_context(nc.sbuf_tensor(f"kb{j}", [128, C8 * 1024], BF16))
                   for j in range(2)]
            vb_ = [ps.enter_context(nc.sbuf_tensor(f"vb{j}", [128, C8 * 1024], BF16))
                   for j in range(2)]
            Oacc = [ps.enter_context(nc.sbuf_tensor(f"Oa{m}", [128, 1024], F32))
                    for m in range(TPC)]
            Pb = [ps.enter_context(nc.sbuf_tensor(f"Pb{j}", [128, 1024], BF16))
                  for j in range(2)]
            Pt = [ps.enter_context(nc.sbuf_tensor(f"Pt{j}", [128, 1024], BF16))
                  for j in range(2)]
            mst = ps.enter_context(nc.sbuf_tensor("mst", [128, TPC], F32))
            lst = ps.enter_context(nc.sbuf_tensor("lst", [128, TPC], F32))
            t4 = [ps.enter_context(nc.sbuf_tensor(f"t4{j}", [128, 4], F32))
                  for j in range(2)]
            ps_s = [ps.enter_context(nc.psum_tensor(f"aps{j}", [128, 1024], F32))
                    for j in range(2)]
            ps_pv = ps.enter_context(nc.psum_tensor("appv", [128, 1024], F32))
            ps_pt = ps.enter_context(nc.psum_tensor("appt", [128, 1024], BF16))
            s_kv = K.sem("a_kv")
            s_kvf = K.sem("a_kvf")
            s_s = [K.sem(f"a_s{j}") for j in range(2)]
            s_sr = [K.sem(f"a_sr{j}") for j in range(2)]
            s_m = K.sem("a_m")
            s_e = [K.sem(f"a_e{j}") for j in range(2)]
            s_tr = [K.sem(f"a_tr{j}") for j in range(2)]
            s_tc = [K.sem(f"a_tc{j}") for j in range(2)]
            s_av = [K.sem(f"a_av{j}") for j in range(2)]
            s_pv = K.sem("a_pv")
            s_o = K.sem("a_o")
            def initO(v):
                for m in range(TPC):
                    v.memset(Oacc[m][:], 0.0)
                v.memset(mst[:], NEG)
                v.memset(lst[:], 0.0)
            K.em("v", initO)
            K.inc("v", s_o)
            K.wait("t", s_o)

            npair = [0]
            use_s = [0, 0]
            use_pb = [0, 0]
            use_pt = [0, 0]
            use_pv = [0]
            use_t4 = [0, 0]

            def attn_pair(m, b, buf, diag, blk_loads):
                if ASTAGE < 1:
                    return
                par = npair[0] % 2
                npair[0] += 1
                K.wait("t", s_kv, blk_loads * 16)
                if use_s[par] > 0:
                    K.wait("t", s_sr[par], use_s[par] * (2 if ASTAGE >= 3 else 1))
                use_s[par] += 1
                def smm(t, m=m, buf=buf, par=par):
                    for c in range(C8):
                        for hh in range(2):
                            t.matmul(ps_s[par][:, hh * 512:(hh + 1) * 512],
                                     qres[c][:, m * 128:(m + 1) * 128],
                                     kb_[buf][:, c * 1024 + hh * 512:c * 1024 + (hh + 1) * 512],
                                     start=(c == 0), stop=(c == C8 - 1))
                K.em("t", smm)
                K.inc("t", s_s[par])
                if ASTAGE < 2:
                    K.em("t", lambda t, par=par: None)
                    K.wait("v", s_s[par])
                    def dummy_rd(v, par=par):
                        v.tensor_copy(t4[par][:, 0:1], ps_s[par][:, 0:1])
                    K.em("v", dummy_rd)
                    K.inc("v", s_sr[par])
                    return
                K.wait("v", s_s[par])
                if use_t4[par] > 0:
                    K.wait("v", s_e[par], use_t4[par])
                use_t4[par] += 1
                def stt(v, m=m, par=par, diag=diag):
                    if diag:
                        v.tensor_add(ps_s[par][:, :], ps_s[par][:, :], trim[:, :])
                    v.reduce_max(t4[par][:, 0:1], ps_s[par][:, :], axis=AX)
                    v.tensor_max(t4[par][:, 1:2], t4[par][:, 0:1], mst[:, m:m + 1])
                    v.tensor_scalar(t4[par][:, 2:3], t4[par][:, 1:2], -1.0, None, ALU.mult)
                K.em("v", stt)
                K.inc("v", s_m)
                K.inc("v", s_sr[par])
                if ASTAGE < 3:
                    return
                K.wait("s", s_m)
                if use_pb[par] > 0 and ASTAGE >= 4:
                    K.wait("s", s_tr[par], use_pb[par])
                use_pb[par] += 1
                def exx(s, m=m, par=par):
                    s.activation(t4[par][:, 3:4], mst[:, m:m + 1], AF.Exp,
                                 bias=t4[par][:, 2:3])
                    s.activation(Pb[par][:, :], ps_s[par][:, :], AF.Exp,
                                 bias=t4[par][:, 2:3], accum_out=t4[par][:, 0:1])
                K.em("s", exx)
                K.inc("s", s_e[par])
                K.inc("s", s_sr[par])
                K.wait("v", s_e[par])
                def lmu(v, m=m, par=par):
                    v.scalar_tensor_tensor(lst[:, m:m + 1], lst[:, m:m + 1], t4[par][:, 3:4],
                                           t4[par][:, 0:1], ALU.mult, ALU.add)
                    v.tensor_copy(mst[:, m:m + 1], t4[par][:, 1:2])
                K.em("v", lmu)
                K.inc("v", s_m)
                if ASTAGE < 4:
                    return
                K.wait("t", s_e[par])
                if use_pt[par] > 0 and ASTAGE >= 5:
                    K.wait("t", s_av[par], use_pt[par])
                K.wait("t", s_tc[0], s_tc[0].n)
                K.wait("t", s_tc[1], s_tc[1].n)
                use_pt[par] += 1
                def trr(t, par=par):
                    for jj in range(8):
                        t.transpose(ps_pt[:, jj * 128:(jj + 1) * 128],
                                    Pb[par][:, jj * 128:(jj + 1) * 128], idbf[:])
                K.em("t", trr)
                K.inc("t", s_tr[par])
                K.wait("s", s_tr[par])
                def tc1(s, par=par):
                    s.copy(Pt[par][:, :], ps_pt[:, :])
                K.em("s", tc1)
                K.inc("s", s_tc[par])
                K.inc("s", s_tc[par])
                if ASTAGE < 5:
                    return
                K.wait("t", s_tc[par])
                if use_pv[0] > 0:
                    K.wait("t", s_o, use_pv[0])
                use_pv[0] += 1
                def avv(t, buf=buf, par=par):
                    for jj in range(8):
                        for hh in range(2):
                            t.matmul(ps_pv[:, hh * 512:(hh + 1) * 512],
                                     Pt[par][:, jj * 128:(jj + 1) * 128],
                                     vb_[buf][:, jj * 1024 + hh * 512:jj * 1024 + (hh + 1) * 512],
                                     start=(jj == 0), stop=(jj == 7))
                K.em("t", avv)
                K.inc("t", s_pv)
                K.inc("t", s_av[par])
                K.wait("v", s_pv)
                def ouu(v, m=m, par=par):
                    v.scalar_tensor_tensor(Oacc[m][:], Oacc[m][:], t4[par][:, 3:4],
                                           ps_pv[:, :], ALU.mult, ALU.add)
                K.em("v", ouu)
                K.inc("v", s_o)

            nload = [0]
            for b in range(TPC):
                buf = b % 2
                if b >= 2:
                    K.wait("y", s_kvf, b - 1)
                def ldb(y, b=b, buf=buf):
                    for jj in range(8):
                        y.dma_start(out=kb_[buf][:, :].rearrange("p (c n) -> p c n", c=C8)
                                    [:, :, jj * 128:(jj + 1) * 128],
                                    in_=cc_k_out[jj * D:(jj + 1) * D, b * 128:(b + 1) * 128]
                                    .rearrange("(c p) n -> c p n", p=128).transpose([1, 0, 2])) \
                         .then_inc(s_kv.h, 16)
                        y.dma_start(out=vb_[buf][:, jj * 1024:(jj + 1) * 1024],
                                    in_=ccvo[jj * SC + b * 128:jj * SC + (b + 1) * 128, :]) \
                         .then_inc(s_kv.h, 16)
                K.em("y", ldb)
                s_kv.n += 16 * 16
                nload[0] += 16
                for m in range(b, TPC):
                    attn_pair(m, b, buf, m == b, nload[0])
                K.inc("t", s_kvf)
                K.flush()
            s_n2 = K.sem("a_n2"); s_t2 = K.sem("a_t2"); s_c2 = K.sem("a_c2")
            s_o2 = K.sem("a_o2")
            for m in range(TPC):
                K.wait("v", s_o)
                K.wait("v", s_c2, s_c2.n)
                K.wait("v", s_tr[0], s_tr[0].n)
                K.wait("v", s_tr[1], s_tr[1].n)
                def onr(v, m=m):
                    v.reciprocal(t4[0][:, 0:1], lst[:, m:m + 1])
                    v.tensor_scalar(Pb[0][:, :], Oacc[m][:], t4[0][:, 0:1], None, ALU.mult)
                K.em("v", onr)
                K.inc("v", s_n2)
                K.wait("t", s_n2)
                K.wait("t", s_c2, s_c2.n)
                K.wait("t", s_tc[0], s_tc[0].n)
                K.wait("t", s_tc[1], s_tc[1].n)
                def otr(t):
                    for c in range(C8):
                        t.transpose(ps_pt[:, c * 128:(c + 1) * 128],
                                    Pb[0][:, c * 128:(c + 1) * 128], idbf[:])
                K.em("t", otr)
                K.inc("t", s_t2)
                K.wait("v", s_t2)
                K.wait("v", s_o2, s_o2.n)
                def ocp(v, m=m):
                    v.tensor_copy(otbuf[:, :], ps_pt[:, :])
                K.em("v", ocp)
                K.inc("v", s_c2)
                K.wait("y", s_c2)
                def ost(y, m=m):
                    y.dma_start(out=s_ot[:, m * 128:(m + 1) * 128]
                                .rearrange("(c p) n -> c p n", p=128).transpose([1, 0, 2]),
                                in_=otbuf[:, :].rearrange("p (c n) -> p c n", c=C8)) \
                     .then_inc(s_o2.h, 16)
                K.em("y", ost)
                s_o2.n += 16
            K.barrier("attn_done")
        ps34.close()

        with contextlib.ExitStack() as ps:
            lbB = ps.enter_context(nc.sbuf_tensor("mtb", [LORA_R, D], BF16))
            laT = ps.enter_context(nc.sbuf_tensor("mta", [LORA_R, D], BF16))
            wpT = [ps.enter_context(nc.sbuf_tensor(f"mtt{c}", [128, D], F32))
                   for c in range(C8)]
            wmp = [ps.enter_context(nc.sbuf_tensor(f"mtm{c}", [128, D], BF16))
                   for c in range(C8)]
            MT = [ps.enter_context(nc.sbuf_tensor(f"mtn{c}", [128, D], BF16))
                  for c in range(C8)]
            x3sp = [ps.enter_context(nc.sbuf_tensor(f"x3sp{c}", [128, 512], F32))
                    for c in range(C8)]
            otsp = [ps.enter_context(nc.sbuf_tensor(f"otsp{c}", [128, 512], BF16))
                    for c in range(C8)]
            psl = ps.enter_context(nc.psum_tensor("mtp", [128, 512], F32))
            mm = MM(ps)
            m1 = K.sem("mt1"); m2 = K.sem("mt2"); m3 = K.sem("mt3"); m1b = K.sem("mt1b")
            s_x = K.sem("pj_x"); s_st = K.sem("pj_st")
            def ldm(g):
                g.dma_start(out=lbB[:, :], in_=lp_b[:, :]).then_inc(m1.h, 16)
                g.dma_start(out=laT[:, :], in_=lp_aT[:, :]).then_inc(m1.h, 16)
                for c in range(C8):
                    g.dma_start(out=MT[c][:, :], in_=wproj_nat[c * 128:(c + 1) * 128, :]) \
                     .then_inc(m1.h, 16)
            K.em("g", ldm)
            m1.n += 16 * 10
            def ldm2(y):
                for c in range(C8):
                    y.dma_start(out=wpT[c][:, :], in_=wproj_T[c * 128:(c + 1) * 128, :]) \
                     .then_inc(m1b.h, 16)
            K.em("y", ldm2)
            m1b.n += 16 * 8
            K.wait("t", m1)
            K.wait("t", m1b)
            for n in range(C8):
                for (o, w) in slc(D):
                    K.wait("t", m3, m3.n)
                    def mmi(t, n=n, o=o, w=w):
                        t.matmul(psl[:, 0:w], lbB[:, n * 128:(n + 1) * 128],
                                 laT[:, o:o + w], start=True, stop=True)
                    K.em("t", mmi)
                    K.inc("t", m2)
                    K.wait("v", m2)
                    def evi(v, n=n, o=o, w=w):
                        v.tensor_add(wmp[n][:, o:o + w], psl[:, 0:w], wpT[n][:, o:o + w])
                    K.em("v", evi)
                    K.inc("v", m3)
            for c in range(C8):
                for (o, w) in slc(D):
                    K.wait("t", m3, m3.n)
                    def mmj(t, c=c, o=o, w=w):
                        for k in range(C8):
                            t.matmul(psl[:, 0:w], MT[k][:, c * 128:(c + 1) * 128],
                                     wmp[k][:, o:o + w], start=(k == 0), stop=(k == C8 - 1))
                    K.em("t", mmj)
                    K.inc("t", m2)
                    K.wait("v", m2)
                    def evj(v, c=c, o=o, w=w):
                        v.tensor_copy(wpT[c][:, o:o + w], psl[:, 0:w])
                    K.em("v", evj)
                    K.inc("v", m3)
            K.wait("v", m3)
            def cstm(v):
                for c in range(C8):
                    v.tensor_copy(MT[c][:, :], wpT[c][:, :])
            K.em("v", cstm)
            K.inc("v", m3)
            K.wait("t", m3)
            for sp in range(NSP2):
                c0 = sp * 512
                if sp > 0:
                    K.wait("y", s_st, sp * 16 * C8)
                def ldx(y, c0=c0):
                    for c in range(C8):
                        y.dma_start(out=x3sp[c][:, :],
                                    in_=s_x2[c * 128:(c + 1) * 128, c0:c0 + 512]) \
                         .then_inc(s_x.h, 16)
                        y.dma_start(out=otsp[c][:, :],
                                    in_=s_ot[c * 128:(c + 1) * 128, c0:c0 + 512]) \
                         .then_inc(s_x.h, 16)
                K.em("y", ldx)
                s_x.n += 32 * C8
                K.wait("v", s_x)
                K.wait("t", s_x)
                def ep_pj(v, o, off, w, psu, c0=c0):
                    v.scalar_tensor_tensor(x3sp[o][:, 0:w], psu[:, 0:w], mc("ga1", o),
                                           x3sp[o][:, 0:w], ALU.mult, ALU.add)
                mm.stage(C8, 512, C8,
                         lambda c, o: MT[c][:, o * 128:(o + 1) * 128],
                         lambda c, off, w: otsp[c][:, off:off + w], ep_pj)
                mm.drain("y")
                def stx(y, c0=c0):
                    for c in range(C8):
                        y.dma_start(out=s_x3[c * 128:(c + 1) * 128, c0:c0 + 512],
                                    in_=x3sp[c][:, :]).then_inc(s_st.h, 16)
                K.em("y", stx)
                s_st.n += 16 * C8
                K.flush()
            K.barrier("proj_done")

        h4res = [sbp(f"h4res{c}", (128, SC), BF16) for c in range(C8)]
        cwrow = sbp("cwrow", (NEXP, SC))
        with contextlib.ExitStack() as ps:
            x3sp = [ps.enter_context(nc.sbuf_tensor(f"g_x{c}", [128, 512], F32))
                    for c in range(C8)]
            zsp = [ps.enter_context(nc.sbuf_tensor(f"g_z{c}", [128, 512], F32))
                   for c in range(C8)]
            gwsb = [ps.enter_context(nc.sbuf_tensor(f"g_w{c}", [128, NEXP], F32))
                    for c in range(C8)]
            dsb = ps.enter_context(nc.sbuf_tensor("g_d", [NEXP, 1], F32))
            lrow = ps.enter_context(nc.sbuf_tensor("g_l", [NEXP, 512], F32))
            tokl = ps.enter_context(nc.sbuf_tensor("g_tl", [128, NEXP], F32))
            tokg = ps.enter_context(nc.sbuf_tensor("g_tg", [128, NEXP], F32))
            tokm = ps.enter_context(nc.sbuf_tensor("g_tm", [128, NEXP], F32))
            t8 = ps.enter_context(nc.sbuf_tensor("g_t8", [128, 8], F32))
            plg = ps.enter_context(nc.psum_tensor("g_pl", [NEXP, 512], F32))
            plt = ps.enter_context(nc.psum_tensor("g_pt", [128, NEXP], F32))
            lnh = make_ln(ps)
            sg = [K.sem(f"g_{k}") for k in range(8)]
            s_x = K.sem("g_x")
            def ldgw(y):
                for c in range(C8):
                    y.dma_start(out=gwsb[c][:, :], in_=gwi[c * 128:(c + 1) * 128, :]) \
                     .then_inc(sg[0].h, 16)
            K.em("y", ldgw)
            sg[0].n += 16 * C8
            K.wait("t", sg[0])
            def dmm(t):
                for c in range(C8):
                    t.matmul(plg[0:NEXP, 0:1], gwsb[c][:, :], mc("sb1", c),
                             start=(c == 0), stop=(c == C8 - 1))
            K.em("t", dmm)
            K.inc("t", sg[1])
            K.wait("v", sg[1])
            def dcp(v):
                v.tensor_copy(dsb[:, :], plg[0:NEXP, 0:1])
                for c in range(C8):
                    v.tensor_scalar(gwsb[c][:, :], gwsb[c][:, :], mc("cb1", c), None, ALU.mult)
            K.em("v", dcp)
            K.inc("v", sg[2])
            K.wait("t", sg[2])
            for sp in range(NSP2):
                c0 = sp * 512
                if sp > 0:
                    K.wait("y", lnh["sap"])
                    K.wait("y", lnh["ssq"])
                    K.wait("y", lnh["smm"])
                    K.wait("y", sg[3])
                def ldx(y, c0=c0):
                    for c in range(C8):
                        y.dma_start(out=x3sp[c][:, :],
                                    in_=s_x3[c * 128:(c + 1) * 128, c0:c0 + 512]) \
                         .then_inc(s_x.h, 16)
                K.em("y", ldx)
                s_x.n += 16 * C8
                for e in ("v", "s", "t"):
                    K.wait(e, s_x)
                emit_ln(lnh, [(x3sp[c], 0) for c in range(C8)], 512,
                        [(h4res[c], c0) for c in range(C8)], "cb1", "sb1",
                        zout=[(zsp[c], 0) for c in range(C8)])
                K.wait("t", lnh["sap"])
                K.wait("t", sg[6], sg[6].n)
                def lmm(t):
                    for c in range(C8):
                        t.matmul(plg[:, :], gwsb[c][:, :], zsp[c][:, :],
                                 start=(c == 0), stop=(c == C8 - 1))
                K.em("t", lmm)
                K.inc("t", sg[3])
                K.wait("v", sg[3])
                def lep(v):
                    v.tensor_scalar(lrow[:, :], plg[:, :], dsb[0:NEXP, 0:1], None, ALU.add)
                K.em("v", lep)
                K.inc("v", sg[4])
                for tt in range(4):
                    K.wait("t", sg[4])
                    K.wait("t", sg[6], sg[6].n)
                    def ltr(t, tt=tt):
                        t.transpose(plt[:, :], lrow[:, tt * 128:(tt + 1) * 128], idf[0:8, 0:8])
                    K.em("t", ltr)
                    K.inc("t", sg[5])
                    K.wait("v", sg[5])
                    def tp1(v):
                        v.tensor_copy(tokl[:, :], plt[:, :])
                        v.reduce_max(t8[:, 0:1], tokl[:, :], axis=AX)
                        v.tensor_scalar(t8[:, 1:2], t8[:, 0:1], -1.0, None, ALU.mult)
                    K.em("v", tp1)
                    K.inc("v", sg[6])
                    K.wait("s", sg[6])
                    def tp2(s):
                        s.activation(tokg[:, :], tokl[:, :], AF.Exp, bias=t8[:, 1:2],
                                     accum_out=t8[:, 2:3])
                    K.em("s", tp2)
                    K.inc("s", sg[7])
                    K.wait("v", sg[7])
                    def tp3(v):
                        v.reciprocal(t8[:, 3:4], t8[:, 2:3])
                        v.tensor_scalar(tokg[:, :], tokg[:, :], t8[:, 3:4], None, ALU.mult)
                        v.max(out=tokm[:, :], in_=tokg[:, :])
                        v.tensor_add(t8[:, 2:3], tokm[:, 0:1], tokm[:, 1:2])
                        v.tensor_scalar(t8[:, 2:3], t8[:, 2:3], 1e-8, None, ALU.add)
                        v.reciprocal(t8[:, 2:3], t8[:, 2:3])
                        v.tensor_mul(t8[:, 4:5], tokm[:, 0:1], t8[:, 2:3])
                        v.tensor_mul(t8[:, 5:6], tokm[:, 1:2], t8[:, 2:3])
                        v.tensor_scalar(tokl[:, :], tokg[:, :], tokm[:, 0:1], t8[:, 4:5],
                                        ALU.is_equal, ALU.mult)
                        v.tensor_scalar(tokg[:, :], tokg[:, :], tokm[:, 1:2], t8[:, 5:6],
                                        ALU.is_equal, ALU.mult)
                        v.tensor_add(tokl[:, :], tokl[:, :], tokg[:, :])
                    K.em("v", tp3)
                    K.inc("v", sg[6])
                    K.wait("t", sg[6])
                    def ctr(t, tt=tt):
                        t.transpose(plg[0:NEXP, 0:128], tokl[:, :], idf[:])
                    K.em("t", ctr)
                    K.inc("t", sg[5])
                    K.wait("v", sg[5])
                    def ccp(v, tt=tt, c0=c0):
                        v.tensor_copy(cwrow[:, c0 + tt * 128:c0 + (tt + 1) * 128],
                                      plg[0:NEXP, 0:128])
                    K.em("v", ccp)
                    K.inc("v", sg[6])
                K.flush()
            K.barrier("gate_done")

        with contextlib.ExitStack() as ps:
            w1b = [ps.enter_context(nc.sbuf_tensor(f"e_w1{c}", [128, 1024], BF16))
                   for c in range(C8)]
            w2b = [ps.enter_context(nc.sbuf_tensor(f"e_w2{c}", [128, 1024], BF16))
                   for c in range(C8)]
            hebf = [ps.enter_context(nc.sbuf_tensor(f"e_h{c}", [128, SC], BF16))
                    for c in range(C8)]
            cwbf = ps.enter_context(nc.sbuf_tensor("e_cw", [128, SC], F32))
            tsb = [ps.enter_context(nc.sbuf_tensor(f"e_t{j}", [128, SC], F32))
                   for j in range(2)]
            tmp4 = ps.enter_context(nc.sbuf_tensor("e_tm", [128, 512], F32))
            b2sb = ps.enter_context(nc.sbuf_tensor("e_b2", [128, NEXP * C8], F32))
            selb = ps.enter_context(nc.sbuf_tensor("e_sel", [NEXP, 128], F32))
            ridx = ps.enter_context(nc.sbuf_tensor("e_ri", [NEXP, 128], F32))
            pb4 = [ps.enter_context(nc.psum_tensor(f"e_p{j}", [128, 512], F32))
                   for j in range(4)]
            pcw = ps.enter_context(nc.psum_tensor("e_pc", [128, 512], F32))
            sA = K.sem("e_a"); sB = K.sem("e_b"); sC = K.sem("e_c")
            sD = K.sem("e_d"); sE = K.sem("e_e")
            sW1 = K.sem("e_w1s"); sW2 = K.sem("e_w2s")
            sm4 = [K.sem(n) for n in ("mm_s0", "mm_s1", "a_s0", "a_s1")]
            se4 = [K.sem(n) for n in ("mm_e0", "mm_e1", "a_sr0", "a_sr1")]
            def ldb2(g):
                g.dma_start(out=b2sb[:, :], in_=b2c[:, :]).then_inc(sA.h, 16)
                g.dma_start(out=ridx[:, :], in_=rowidx8[:, :]).then_inc(sA.h, 16)
            K.em("g", ldb2)
            sA.n += 32

            def stage4(n_out, nin, lhsT_fn, rhs_fn, epil, ep_eng):
                for o in range(n_out):
                    for j in range(NSP2):
                        K.wait("t", se4[j])
                    def mmo(t, o=o, nin=nin):
                        for c in range(nin):
                            lh = lhsT_fn(c, o)
                            for j in range(NSP2):
                                t.matmul(pb4[j][:, :], lh, rhs_fn(c, j),
                                         start=(c == 0), stop=(c == nin - 1))
                    K.em("t", mmo)
                    for j in range(NSP2):
                        K.inc("t", sm4[j])
                    for j in range(NSP2):
                        K.wait(ep_eng, sm4[j])
                        def ep(e, o=o, j=j):
                            epil(e, o, j, pb4[j])
                        K.em(ep_eng, ep)
                        K.inc(ep_eng, se4[j])

            nacc = [0]
            for e in range(NEXP):
                K.wait("v", sE, sE.n)
                K.wait("v", sA, 32)
                def setsel(v, e=e):
                    v.tensor_scalar(selb[:, :], ridx[:, :], float(e), None, ALU.is_equal)
                K.em("v", setsel)
                K.inc("v", sB)
                for sp in range(NSP2):
                    K.wait("t", sB, sB.n)
                    K.wait("t", sD, sD.n)
                    def cbc(t, e=e, sp=sp):
                        t.matmul(pcw[:, :], selb[:, :], cwrow[0:NEXP, sp * 512:(sp + 1) * 512],
                                 start=True, stop=True)
                    K.em("t", cbc)
                    K.inc("t", sE)
                    K.wait("v", sE)
                    def cbv(v, sp=sp):
                        v.tensor_copy(cwbf[:, sp * 512:(sp + 1) * 512], pcw[:, :])
                    K.em("v", cbv)
                    K.inc("v", sD)
                K.wait("t", sD)
                for h in range(2):
                    for j in range(4):
                        K.wait("g", se4[j])
                    def ldw(g, e=e, h=h):
                        for c in range(C8):
                            g.dma_start(out=w1b[c][:, :],
                                        in_=w1e[e, c * 128:(c + 1) * 128,
                                                h * 1024:(h + 1) * 1024]).then_inc(sW1.h, 16)
                        for c in range(C8):
                            g.dma_start(out=w2b[c][:, :],
                                        in_=w2e[e, h * 1024 + c * 128:h * 1024 + (c + 1) * 128,
                                                :]).then_inc(sW2.h, 16)
                    K.em("g", ldw)
                    sW1.n += 16 * C8
                    sW2.n += 16 * C8
                    K.wait("t", sW1)
                    def ep_h(s, o, j, psu, e=e, h=h):
                        s.activation(hebf[o][:, j * 512:(j + 1) * 512], psu[:, :], AF.Relu,
                                     bias=b1sb[:, e * C16 + h * 8 + o:e * C16 + h * 8 + o + 1])
                    stage4(C8, C8,
                           lambda c, o: w1b[c][:, o * 128:(o + 1) * 128],
                           lambda c, j: h4res[c][:, j * 512:(j + 1) * 512],
                           ep_h, "s")
                    K.wait("t", sW2)
                    for j in range(4):
                        K.wait("t", se4[j])
                    par = (e * 2 + h) % 2
                    def ep_o2(v, o, j, psu, e=e, h=h, par=par):
                        d = tsb[par][:, j * 512:(j + 1) * 512]
                        if h == 0:
                            v.tensor_scalar(tmp4[:, :], psu[:, :],
                                            b2sb[:, e * 8 + o:e * 8 + o + 1], None, ALU.add)
                            v.tensor_mul(d, tmp4[:, :], cwbf[:, j * 512:(j + 1) * 512])
                        else:
                            v.tensor_mul(d, psu[:, :], cwbf[:, j * 512:(j + 1) * 512])
                    for o in range(C8):
                        for j in range(NSP2):
                            K.wait("t", se4[j])
                        def mmo(t, o=o):
                            for c in range(C8):
                                lh = w2b[c][:, o * 128:(o + 1) * 128]
                                for j in range(NSP2):
                                    t.matmul(pb4[j][:, :], lh,
                                             hebf[c][:, j * 512:(j + 1) * 512],
                                             start=(c == 0), stop=(c == C8 - 1))
                        K.em("t", mmo)
                        for j in range(NSP2):
                            K.inc("t", sm4[j])
                        K.wait("v", sC, max(0, sC.n - 16))
                        for j in range(NSP2):
                            K.wait("v", sm4[j])
                            def ep(v, o=o, j=j):
                                ep_o2(v, o, j, pb4[j])
                            K.em("v", ep)
                            K.inc("v", se4[j])
                        K.wait("g", se4[0], se4[0].n)
                        K.wait("g", se4[1], se4[1].n)
                        K.wait("g", se4[2], se4[2].n)
                        K.wait("g", se4[3], se4[3].n)
                        K.wait("g", sC)
                        first = (e == 0 and h == 0)
                        def acc(g, o=o, par=par, first=first):
                            g.dma_start(out=s_macc[o * 128:(o + 1) * 128, :],
                                        in_=tsb[par][:, :],
                                        **({} if first else {"accum_op": ALU.add})) \
                             .then_inc(sC.h, 16)
                        K.em("g", acc)
                        sC.n += 16
                    K.flush()
            K.barrier("moe_done")

        if DBG_TAPS:
            tapmap = {"x2": s_x2, "x3": s_x3, "xa": s_xa, "macc": s_macc,
                      "kin": cc_k_in, "kout": cc_k_out, "vin": ccvi,
                      "vout": ccvo, "bno": cc_bn_out, "ot": s_ot,
                      "wm": s_wm, "mt": s_wm,
                      "wqin": s_wqin, "mod": s_mod, "dbgq": s_dbgq}
            s_dbg = K.sem("md_st")
            if "vecs" in DBG_TAPS:
                ov = nc.dram_tensor("dbg_vecs", [128, 144], F32, kind="ExternalOutput").ap()
                def dbgv(g, ov=ov):
                    g.dma_start(out=ov[:, :], in_=vecs[:, :]).then_inc(s_dbg.h, 16)
                K.em("g", dbgv)
                s_dbg.n += 16
            for nm in DBG_TAPS:
                if nm == "vecs":
                    continue
                t = tapmap[nm]
                o = nc.dram_tensor("dbg_" + nm, list(t.tensor.shape), t.tensor.dtype,
                                   kind="ExternalOutput").ap()
                def dbgcp(g, t=t, o=o):
                    g.dma_start(out=o[:, :], in_=t[:, :]).then_inc(s_dbg.h, 16)
                K.em("g", dbgcp)
                s_dbg.n += 16
            K.wait("g", s_dbg)
        with contextlib.ExitStack() as ps:
            x3sp = [ps.enter_context(nc.sbuf_tensor(f"f_x{c}", [128, 512], F32))
                    for c in range(C8)]
            msp = [ps.enter_context(nc.sbuf_tensor(f"f_m{c}", [128, 512], F32))
                   for c in range(C8)]
            xtok = ps.enter_context(nc.sbuf_tensor("f_tk", [128, 1024], F32))
            pft = ps.enter_context(nc.psum_tensor("f_pt", [128, 1024], F32))
            f1 = K.sem("f_1"); f2 = K.sem("f_2"); f3 = K.sem("f_3"); f4 = K.sem("f_4")
            fout = K.sem("f_out")
            for sp in range(NSP2):
                c0 = sp * 512
                if sp > 0:
                    K.wait("y", f2, sp)
                def ldf(y, c0=c0):
                    for c in range(C8):
                        y.dma_start(out=x3sp[c][:, :],
                                    in_=s_x3[c * 128:(c + 1) * 128, c0:c0 + 512]) \
                         .then_inc(f1.h, 16)
                        y.dma_start(out=msp[c][:, :],
                                    in_=s_macc[c * 128:(c + 1) * 128, c0:c0 + 512]) \
                         .then_inc(f1.h, 16)
                K.em("y", ldf)
                f1.n += 32 * C8
                K.wait("v", f1)
                def fin(v, c0=c0):
                    for c in range(C8):
                        v.tensor_add(msp[c][:, :], msp[c][:, :], h4res[c][:, c0:c0 + 512])
                        v.scalar_tensor_tensor(x3sp[c][:, :], msp[c][:, :], mc("gb1", c),
                                               x3sp[c][:, :], ALU.mult, ALU.add)
                K.em("v", fin)
                for tt in range(4):
                    K.inc("v", f3)
                    K.wait("t", f3)
                    K.wait("t", f4, f4.n)
                    def ftr(t, tt=tt):
                        for c in range(C8):
                            t.transpose(pft[:, c * 128:(c + 1) * 128],
                                        x3sp[c][:, tt * 128:(tt + 1) * 128], idf[:])
                    K.em("t", ftr)
                    K.inc("t", f3)
                    K.wait("v", f3)
                    def fcp(v):
                        v.tensor_copy(xtok[:, :], pft[:, :])
                    K.em("v", fcp)
                    K.inc("v", f4)
                    K.wait("y", f4)
                    def fst(y, sp=sp, tt=tt):
                        y.dma_start(out=out[sp * 512 + tt * 128:sp * 512 + (tt + 1) * 128, :],
                                    in_=xtok[:, :]).then_inc(fout.h, 16)
                    K.em("y", fst)
                    fout.n += 16
                    K.wait("v", fout)
                K.inc("y", f2)
                K.flush()
            for e in KB.E:
                K.wait(e, fout)
            K.flush()

    gstack.close()
    return nc


def _host_stage(inputs, S):
    TPC = S // TILE // NCORE
    x = np.ascontiguousarray(inputs["L0_attn_x"].reshape(S, D).astype(np.float32))
    shared = {}
    shared["w_in_T"] = np.ascontiguousarray(inputs["L0_attn_w_in"].T)
    shared["w_x_T"] = np.ascontiguousarray(inputs["L0_attn_w_x"].T)
    shared["w_out_T"] = np.ascontiguousarray(inputs["L0_attn_w_out"].T)
    shared["wr_T"] = np.ascontiguousarray(inputs["L0_ffn_W_r"].T)
    shared["wk_T"] = np.ascontiguousarray(inputs["L0_ffn_W_k"].T)
    shared["wv_T"] = np.ascontiguousarray(inputs["L0_ffn_W_v"].T)
    shared["wattn_T"] = np.ascontiguousarray(inputs["L1_attn_W_attn"].T)
    shared["wproj_nat"] = np.ascontiguousarray(inputs["L1_attn_W_proj"])
    shared["wproj_T"] = np.ascontiguousarray(inputs["L1_attn_W_proj"].T)
    shared["la_aT"] = np.ascontiguousarray(inputs["L1_attn_lora_attn_A"].T)
    shared["la_b"] = np.ascontiguousarray(inputs["L1_attn_lora_attn_B"])
    shared["lp_aT"] = np.ascontiguousarray(inputs["L1_attn_lora_proj_A"].T)
    shared["lp_b"] = np.ascontiguousarray(inputs["L1_attn_lora_proj_B"])
    shared["wmod0"] = np.ascontiguousarray(inputs["L0_wmod"])
    shared["wmod1"] = np.ascontiguousarray(inputs["L1_wmod"])
    shared["c01T"] = np.ascontiguousarray(
        np.concatenate([inputs["L0_c"].reshape(-1), inputs["L1_c"].reshape(-1)])[:, None])
    shared["gwi"] = np.ascontiguousarray(inputs["L1_ffn_gate_W"])
    shared["w1e"] = np.ascontiguousarray(inputs["L1_ffn_W1"])
    shared["w2e"] = np.ascontiguousarray(inputs["L1_ffn_W2"])
    shared["b1c"] = np.ascontiguousarray(
        inputs["L1_ffn_b1"].reshape(NEXP, C16, 128).transpose(2, 0, 1).reshape(128, NEXP * C16))
    shared["b2n"] = np.ascontiguousarray(inputs["L1_ffn_b2"])
    shared["b2c"] = np.ascontiguousarray(
        inputs["L1_ffn_b2"].reshape(NEXP, C8, 128).transpose(2, 0, 1).reshape(128, NEXP * C8))
    cols = []
    cols.append(inputs["L0_attn_b_in"][:DI].reshape(C16, 128).T)
    cols.append(inputs["L0_attn_b_conv"].reshape(C16, 128).T)
    cols.append(inputs["L0_attn_b_x"][:DI].reshape(C16, 128).T)
    cols.append(inputs["L0_attn_b_out"].reshape(C8, 128).T)
    cols.append(inputs["L0_ffn_mu_r"].reshape(C8, 128).T)
    cols.append(inputs["L0_ffn_mu_k"].reshape(C8, 128).T)
    cols.append(inputs["L1_attn_bn_gamma"].reshape(24, 128).T)
    cols.append(inputs["L1_attn_bn_beta"].reshape(24, 128).T)
    vt = np.concatenate(cols, axis=1).astype(np.float32)
    assert vt.shape == (128, 120)
    shared["vtab"] = np.ascontiguousarray(vt)
    shared["wconv_c"] = np.ascontiguousarray(
        inputs["L0_attn_w_conv"][:, 0, :].reshape(C16, 128, 4).transpose(1, 0, 2)
        .reshape(128, C16 * 4))
    shared["id_bf"] = np.eye(128, dtype=np.float32).astype(ml_dtypes.bfloat16)
    shared["id_f32"] = np.eye(128, dtype=np.float32)
    shared["rowidx8"] = np.repeat(np.arange(NEXP, dtype=np.float32)[:, None], 128, axis=1)
    maps = []
    for i in range(NCORE):
        m = dict(shared)
        rows = []
        for mslot in range(TPC):
            g = mslot * NCORE + i
            lo = g * TILE - HALO
            if lo < 0:
                rows.append(np.zeros((HALO, D), np.float32))
                rows.append(x[0:g * TILE + TILE])
            else:
                rows.append(x[lo:g * TILE + TILE])
        xs = np.concatenate(rows, axis=0)
        m["xsh"] = np.ascontiguousarray(xs.T)
        f = np.arange(1024)[None, :]
        pp = np.arange(128)[:, None]
        m["trimask"] = np.where(f <= pp + 128 * i, 0.0, NEG).astype(np.float32)
        m["haloflag"] = np.full((128, 1), 0.0 if i == 0 else 1.0, np.float32)
        maps.append(m)
    return maps


def kernel(**inputs):
    S = int(np.prod(inputs["L0_attn_x"].shape[:-1]))
    maps = _host_stage(inputs, S)
    nc = build_nc(S)
    res = run_bass_kernel_spmd(nc, maps, core_ids=list(range(NCORE)))
    TPC = S // TILE // NCORE
    full = np.empty((S, D), np.float32)
    for i in range(NCORE):
        shard = np.asarray(res.results[i]["out"])
        for mslot in range(TPC):
            g = mslot * NCORE + i
            full[g * TILE:(g + 1) * TILE] = shard[mslot * TILE:(mslot + 1) * TILE]
    return full.reshape(inputs["L0_attn_x"].shape)


# version 11
# speedup vs baseline: 1.0894x; 1.0894x over previous
import contextlib
import numpy as np
import ml_dtypes
import concourse.bass as bass
import concourse.mybir as mybir
from concourse.bass_utils import run_bass_kernel_spmd

F32 = mybir.dt.float32
BF16 = mybir.dt.bfloat16
AF = mybir.ActivationFunctionType
ALU = mybir.AluOpType
AX = mybir.AxisListType.X

NCORE = 8
D = 1024
C8 = 8
DI = 2048
C16 = 16
DX = 3088
NEXP = 8
LORA_R = 64
TILE = 128
HALO = 8
TW = TILE + HALO
NEG = -1.0e30
ASTAGE = 99
DBG_TAPS = ()
EPS_LN = 1e-5
EPS_BIT = 1e-5


class Sem:
    def __init__(self, nc, es, name):
        self.h = es.enter_context(nc.semaphore(name))
        self.n = 0


_SEQ_OPS = frozenset((
    "wait_ge", "sem_inc", "nop", "engine_nop", "maybe_nop_then_inc",
    "maybe_drain_then_inc", "dma_start", "dma_start_transpose",
    "collective_compute", "inc_swdge_sem", "sem_clear", "reg_mov",
))


def _ap_names(args, kw):
    out = []
    for v in list(args) + list(kw.values()):
        t = getattr(v, "tensor", None)
        nm = getattr(t, "name", None)
        if nm is not None:
            out.append(nm)
    return out


class _EngRec:

    def __init__(self, eng, st, autodrain):
        self._eng = eng
        self._st = st
        self._ad = autodrain

    def __getattr__(self, name):
        a = getattr(self._eng, name)
        if not callable(a) or name in _SEQ_OPS or name.startswith("_"):
            return a
        def call(*args, **kw):
            st = self._st
            if self._ad:
                names = _ap_names(args, kw)
                wr = st.setdefault("written", set())
                if any(n in wr for n in names):
                    self._eng.drain()
                    wr.clear()
                if names:
                    wr.add(names[0])
                ao = kw.get("accum_out")
                t = getattr(ao, "tensor", None)
                if t is not None:
                    wr.add(t.name)
            r = a(*args, **kw)
            if r is not None and hasattr(r, "then_inc"):
                st["last"] = r
                st["armed"] = True
            return r
        return call


class KB:
    E = ("t", "v", "s", "g", "y")

    def __init__(self, nc, blk, es):
        self.nc, self.blk, self.es = nc, blk, es
        self.bufs = {e: [] for e in self.E}
        self._sid = 0
        self._pool = {}
        self.st = {e: {"last": None, "armed": False, "guard": None} for e in self.E}

    def sem(self, name=None):
        if name is not None and name in self._pool:
            return self._pool[name]
        self._sid += 1
        s = Sem(self.nc, self.es, f"{name or 'sm'}_{self._sid}")
        if name is not None:
            self._pool[name] = s
        return s

    def em(self, eng, fn):
        self.bufs[eng].append(fn)

    def wait(self, eng, sem, val=None):
        v = sem.n if val is None else val
        if v > 0:
            self.em(eng, lambda e, h=sem.h, v=v: e.wait_ge(h, v))

    def inc(self, eng, sem, k=1):
        st = self.st[eng]
        h = sem.h
        va = sem.n + k
        def go(e, st=st, h=h, k=k, va=va):
            if st["armed"]:
                st["last"].then_inc(h, k)
                st["armed"] = False
                st["guard"] = (h, va)
            elif st["guard"] is not None:
                gh, gv = st["guard"]
                e.wait_ge(gh, gv)
                e.sem_inc(h, k)
                st["guard"] = (h, va)
            else:
                e.sem_inc(h, k)
                st["guard"] = (h, va)
        self.em(eng, go)
        sem.n += k

    def flush(self):
        m = {"t": self.blk.tensor, "v": self.blk.vector, "s": self.blk.scalar,
             "g": self.blk.gpsimd, "y": self.blk.sync}
        for e in self.E:
            fns = self.bufs[e]
            if fns:
                self.bufs[e] = []
                st = self.st[e]
                ad = e in ("v", "s", "g")
                m[e](lambda eng, fns=fns, st=st, ad=ad:
                     [f(_EngRec(eng, st, ad)) for f in fns])

    def barrier(self, name="bar"):
        s = self.sem("_barrier")
        for e in self.E:
            self.inc(e, s)
        for e in self.E:
            self.wait(e, s)
        self.flush()


def slc(width, step=512):
    out, o = [], 0
    while o < width:
        w = min(step, width - o)
        out.append((o, w))
        o += w
    return out


def row2col(ap2d):
    return ap2d.rearrange("a (p o) -> (a p) o", p=128)


def build_nc(S):
    NT = S // TILE
    TPC = NT // NCORE
    SPT = 4 if TPC % 4 == 0 else (2 if TPC % 2 == 0 else 1)
    NSP = TPC // SPT
    SW = SPT * TW
    SC = TPC * TILE
    NSP2 = SC // 512
    assert S % 1024 == 0 and NT % NCORE == 0 and SC % 512 == 0

    nc = bass.Bass("TRN2", target_bir_lowering=False, detect_race_conditions=False)
    gstack = contextlib.ExitStack()

    def din(name, shape, dt=F32):
        return nc.dram_tensor(name, list(shape), dt, kind="ExternalInput").ap()

    def scr(name, shape, dt):
        return nc.dram_tensor(name, list(shape), dt, kind="Internal").ap()

    xsh = din("xsh", (D, TPC * TW))
    w_in_T = din("w_in_T", (D, 2 * DI))
    w_x_T = din("w_x_T", (DI, DX))
    w_out_T = din("w_out_T", (DI, D))
    wr_T = din("wr_T", (D, D))
    wk_T = din("wk_T", (D, D))
    wv_T = din("wv_T", (D, D))
    wattn_T = din("wattn_T", (D, 3 * D))
    wproj_nat = din("wproj_nat", (D, D))
    wproj_T = din("wproj_T", (D, D))
    la_aT = din("la_aT", (LORA_R, D))
    la_b = din("la_b", (LORA_R, 3 * D))
    lp_aT = din("lp_aT", (LORA_R, D))
    lp_b = din("lp_b", (LORA_R, D))
    wmod0 = din("wmod0", (D, 6 * D))
    wmod1 = din("wmod1", (D, 6 * D))
    c01T = din("c01T", (2 * D, 1))
    gwi = din("gwi", (D, NEXP))
    w1e = din("w1e", (NEXP, D, DI))
    w2e = din("w2e", (NEXP, DI, D))
    b1c = din("b1c", (128, NEXP * C16))
    b2n = din("b2n", (NEXP, D))
    b2c = din("b2c", (128, NEXP * C8))
    vtab = din("vtab", (128, 120))
    wconv_c = din("wconv_c", (128, C16 * 4))
    id_bf = din("id_bf", (128, 128), BF16)
    id_f32 = din("id_f32", (128, 128))
    trimask = din("trimask", (128, 1024))
    rowidx8 = din("rowidx8", (NEXP, 128))
    haloflag = din("haloflag", (128, 1))
    out = nc.dram_tensor("out", [SC, D], F32, kind="ExternalOutput").ap()

    s_wqin = scr("s_wqin", (D, DI), BF16)
    s_wqx = scr("s_wqx", (DI, DI), BF16)
    s_wqout = scr("s_wqout", (DI, D), BF16)
    s_wr = scr("s_wr", (D, D), BF16)
    s_wk = scr("s_wk", (D, D), BF16)
    s_wv = scr("s_wv", (D, D), BF16)
    s_R = scr("s_R", (3 * D, D), BF16)
    s_wm = scr("s_wm", (D, 3 * D), BF16)
    s_mod = scr("s_mod", (2, 6 * D), F32)
    s_dbgq = scr("s_dbgq", (128, 8), F32)
    s_x2 = scr("s_x2", (D, SC), F32)
    s_x3 = scr("s_x3", (D, SC), F32)
    s_xa = scr("s_xa", (3 * D, SC), BF16)
    s_macc = scr("s_macc", (D, SC), F32)
    s_ot = scr("s_ot", (D, SC), BF16)
    cc_k_in = scr("cc_k_in", (D, SC), BF16)
    cc_k_out = scr("cc_k_out", (NCORE * D, SC), BF16)
    ccvi = scr("ccvi", (SC, D), BF16)
    ccvo = scr("ccvo", (S, D), BF16)
    cc_bn_in = scr("cc_bn_in", (2, 3 * D), F32)
    cc_bn_out = scr("cc_bn_out", (2, 3 * D), F32)

    VT = {}
    _c = 0
    for nm, n in [("b_in", C16), ("b_conv", C16), ("b_x", C16), ("b_out", C8),
                  ("mur", C8), ("muk", C8), ("gma", 24), ("gmb", 24)]:
        VT[nm] = _c
        _c += n
    NVT_IN = _c
    for nm, n in [("murm", C8), ("mukm", C8), ("scl", 4)]:
        VT[nm] = _c
        _c += n
    assert _c <= 144

    def sbp(name, shape, dt=F32):
        return gstack.enter_context(nc.sbuf_tensor(name, list(shape), dt))

    vecs = sbp("vecs", (128, 144))
    wcv = sbp("wcv", (128, C16 * 4))

    def wcv_col(c, k):
        j = c * 4 + k
        return wcv[:, j:j + 1]
    mods = sbp("mods", (128, 96))
    MD = {}
    for j, nm in enumerate(["sa0", "ca0", "ga0", "sb0", "cb0", "gb0",
                            "sa1", "ca1", "ga1", "sb1", "cb1", "gb1"]):
        MD[nm] = j * 8

    def vc(nm, j=0):
        c = VT[nm] + j
        return vecs[:, c:c + 1]

    def mc(nm, j=0):
        c = MD[nm] + j
        return mods[:, c:c + 1]

    b1sb = sbp("b1sb", (128, NEXP * C16))
    idbf = sbp("idbf", (128, 128), BF16)
    idf = sbp("idf", (128, 128))
    trim = sbp("trim", (128, 1024))
    hflag = sbp("hflag", (128, 1))
    ones_r = sbp("ones_r", (1, 128))
    ones_c = sbp("ones_c", (128, 1))

    with nc.Block() as blk:
        K = KB(nc, blk, gstack)

        s0 = K.sem("c_ld")
        def ldc(y):
            y.dma_start(out=vecs[:, 0:NVT_IN], in_=vtab[:, 0:NVT_IN]).then_inc(s0.h, 16)
            y.dma_start(out=wcv[:], in_=wconv_c[:, :]).then_inc(s0.h, 16)
            y.dma_start(out=b1sb[:], in_=b1c[:, :]).then_inc(s0.h, 16)
            y.dma_start(out=idbf[:], in_=id_bf[:, :]).then_inc(s0.h, 16)
            y.dma_start(out=idf[:], in_=id_f32[:, :]).then_inc(s0.h, 16)
            y.dma_start(out=trim[:], in_=trimask[:, :]).then_inc(s0.h, 16)
            y.dma_start(out=hflag[:], in_=haloflag[:, :]).then_inc(s0.h, 16)
        K.em("y", ldc)
        s0.n += 7 * 16
        K.wait("v", s0)
        def ic(v):
            v.memset(ones_r[:], 1.0)
            v.memset(ones_c[:], 1.0)
            v.tensor_scalar(vecs[:, VT["murm"]:VT["murm"] + C8],
                            vecs[:, VT["mur"]:VT["mur"] + C8], -1.0, 1.0, ALU.mult, ALU.add)
            v.tensor_scalar(vecs[:, VT["mukm"]:VT["mukm"] + C8],
                            vecs[:, VT["muk"]:VT["muk"] + C8], -1.0, 1.0, ALU.mult, ALU.add)
        K.em("v", ic)
        K.flush()

        with contextlib.ExitStack() as ps:
            cc = ps.enter_context(nc.sbuf_tensor("md_c", [128, 16], F32))
            sil = ps.enter_context(nc.sbuf_tensor("md_s", [128, 16], F32))
            wmb = [ps.enter_context(nc.sbuf_tensor(f"md_w{j}", [128, 2048], F32))
                   for j in range(2)]
            mrow = ps.enter_context(nc.sbuf_tensor("md_r", [1, 6 * D], F32))
            psr = ps.enter_context(nc.psum_tensor("md_p", [1, 2048], F32))
            sld = K.sem("md_ld")
            ssi = K.sem("md_si")
            smm = K.sem("md_mm")
            sev = K.sem("md_ev")
            sst = K.sem("md_st")
            def ldcc(y):
                for li in range(2):
                    for ch in range(C8):
                        y.dma_start(out=cc[:, li * 8 + ch:li * 8 + ch + 1],
                                    in_=c01T[li * D + ch * 128:li * D + (ch + 1) * 128, 0:1]) \
                         .then_inc(sld.h, 16)
            K.em("y", ldcc)
            sld.n += 16 * 16
            K.wait("s", sld)
            K.em("s", lambda s: s.activation(sil[:, :], cc[:, :], AF.Silu))
            K.inc("s", ssi)
            wld = [0]
            for li, wmap in enumerate([wmod0, wmod1]):
                for grp in range(3):
                    g0 = grp * 2048
                    for ch in range(C8):
                        par = wld[0] % 2
                        if wld[0] >= 2:
                            K.wait("y", smm, wld[0] - 1)
                        def ldw(y, wmap=wmap, ch=ch, g0=g0, par=par):
                            y.dma_start(out=wmb[par][:, :],
                                        in_=wmap[ch * 128:(ch + 1) * 128, g0:g0 + 2048]) \
                             .then_inc(sld.h, 16)
                        K.em("y", ldw)
                        sld.n += 16
                        wld[0] += 1
                        K.wait("t", sld)
                        K.wait("t", ssi)
                        if ch == 0:
                            K.wait("t", sev)
                        def mm(t, ch=ch, li=li, par=par):
                            for o in range(0, 2048, 512):
                                t.matmul(psr[0:1, o:o + 512], sil[:, li * 8 + ch:li * 8 + ch + 1],
                                         wmb[par][:, o:o + 512],
                                         start=(ch == 0), stop=(ch == C8 - 1))
                        K.em("t", mm)
                        K.inc("t", smm)
                    K.wait("v", smm)
                    def ev(v, g0=g0):
                        v.tensor_copy(mrow[0:1, g0:g0 + 2048], psr[0:1, :])
                    K.em("v", ev)
                    K.inc("v", sev)
                K.wait("g", sev)
                def st(g, li=li):
                    g.dma_start(out=s_mod[li:li + 1, :], in_=mrow[0:1, :]).then_inc(sst.h, 16)
                K.em("g", st)
                sst.n += 16
            K.wait("y", sst)
            def ld2(y):
                for li in range(2):
                    for k6 in range(6):
                        for ch in range(C8):
                            col = li * 48 + k6 * 8 + ch
                            y.dma_start(out=mods[:, col:col + 1],
                                        in_=row2col(s_mod[li:li + 1,
                                                          k6 * D + ch * 128:k6 * D + (ch + 1) * 128])) \
                             .then_inc(sld.h, 16)
            K.em("y", ld2)
            sld.n += 96 * 16
            K.wait("v", sld)
            def drv(v):
                for nm in ["ga0", "gb0", "ga1", "gb1"]:
                    b = MD[nm]
                    for ch in range(C8):
                        a = mods[:, b + ch:b + ch + 1]
                        v.tensor_scalar(cc[:, 0:1], a, 0.0, 2.0, ALU.is_ge, ALU.mult)
                        v.tensor_scalar(cc[:, 0:1], cc[:, 0:1], -1.0, None, ALU.add)
                        v.tensor_scalar(cc[:, 1:2], a, -1.0, None, ALU.mult)
                        v.tensor_max(cc[:, 1:2], cc[:, 1:2], a)
                        v.tensor_scalar(cc[:, 1:2], cc[:, 1:2], 0.25, None, ALU.max)
                        v.tensor_mul(a, cc[:, 0:1], cc[:, 1:2])
                for nm in ["ca0", "cb0", "ca1", "cb1"]:
                    b = MD[nm]
                    v.tensor_scalar(mods[:, b:b + 8], mods[:, b:b + 8], 1.0, None, ALU.add)
            K.em("v", drv)
            K.barrier("md_done")

        with contextlib.ExitStack() as ps:
            wbuf = [ps.enter_context(nc.sbuf_tensor(f"q_w{j}", [128, 4096], F32))
                    for j in range(2)]
            qbuf = [ps.enter_context(nc.sbuf_tensor(f"q_q{j}", [128, 2048], BF16))
                    for j in range(2)]
            tbuf = ps.enter_context(nc.sbuf_tensor("q_t", [128, 2048], F32))
            dmy = ps.enter_context(nc.sbuf_tensor("q_d", [128, 4096], F32))
            strip = ps.enter_context(nc.sbuf_tensor("q_s", [128, 16], F32))
            thrt = ps.enter_context(nc.sbuf_tensor("q_thr", [128, 2], F32))
            sc1 = ps.enter_context(nc.sbuf_tensor("q_s1", [1, 1], F32))
            psq = ps.enter_context(nc.psum_tensor("q_p", [128, 8], F32))
            sld = K.sem("q_ld")
            sab = K.sem("q_ab")
            smm = K.sem("q_mm")
            sth = K.sem("q_th")
            sqz = K.sem("q_qz")
            sst = K.sem("q_st")
            srdn = K.sem("q_rdn")
            for wi, (src, nch, fc, uc, dst, sidx) in enumerate([
                    (w_in_T, C8, 2 * DI, DI, s_wqin, 0),
                    (w_x_T, C16, DX, DI, s_wqx, 1),
                    (w_out_T, C16, D, D, s_wqout, 2)]):
                for ch in range(nch):
                    par = ch % 2
                    if ch >= 2:
                        K.wait("y", sab, sab.n - 1)
                    elif wi > 0:
                        K.wait("y", sqz)
                    def ld(y, src=src, ch=ch, fc=fc, par=par):
                        y.dma_start(out=wbuf[par][:, 0:fc],
                                    in_=src[ch * 128:(ch + 1) * 128, 0:fc]).then_inc(sld.h, 16)
                    K.em("y", ld)
                    sld.n += 16
                    K.wait("s", sld)
                    if wi > 0 and ch == 0:
                        K.wait("s", srdn, wi)
                    def ab(s, ch=ch, fc=fc, par=par):
                        s.activation(dmy[:, 0:fc], wbuf[par][:, 0:fc], AF.Abs,
                                     accum_out=strip[:, ch:ch + 1])
                    K.em("s", ab)
                    K.inc("s", sab)
                K.wait("v", sab)
                def rd(v, nch=nch):
                    v.reduce_sum(strip[:, 0:1], strip[:, 0:nch], axis=AX)
                K.em("v", rd)
                K.inc("v", srdn)
                srd = sab
                K.inc("v", srd)
                K.wait("t", srd)
                def mm(t):
                    t.matmul(psq[0:1, 0:1], strip[:, 0:1], ones_c[:], start=True, stop=True)
                K.em("t", mm)
                K.inc("t", smm)
                K.wait("v", smm)
                def cp1(v):
                    v.tensor_copy(sc1[:, :], psq[0:1, 0:1])
                K.em("v", cp1)
                K.inc("v", smm)
                K.wait("t", smm)
                def mm2(t):
                    t.matmul(psq[:, 1:2], ones_r[:], sc1[:, :], start=True, stop=True)
                K.em("t", mm2)
                K.inc("t", smm)
                K.wait("v", smm)
                nelem = float(nch * 128 * fc)
                def thr(v, nelem=nelem, sidx=sidx):
                    v.tensor_scalar(vc("scl", sidx), psq[:, 1:2], 1.0 / nelem, EPS_BIT,
                                    ALU.mult, ALU.max)
                    v.tensor_scalar(thrt[:, 0:1], vc("scl", sidx), 0.5, None, ALU.mult)
                    v.tensor_scalar(thrt[:, 1:2], thrt[:, 0:1], -1.0, None, ALU.mult)
                K.em("v", thr)
                K.inc("v", sth)
                for ch in range(nch):
                    par = ch % 2
                    if ch >= 2:
                        K.wait("y", sqz, sqz.n - 1)
                    else:
                        K.wait("y", sab)
                    def ld(y, src=src, ch=ch, uc=uc, par=par):
                        y.dma_start(out=wbuf[par][:, 0:uc],
                                    in_=src[ch * 128:(ch + 1) * 128, 0:uc]).then_inc(sld.h, 16)
                    K.em("y", ld)
                    sld.n += 16
                    K.wait("v", sld)
                    K.wait("v", sth)
                    if ch >= 2:
                        K.wait("v", sst, sst.n - 16)
                    def qz(v, uc=uc, par=par):
                        v.tensor_scalar(tbuf[:, 0:uc], wbuf[par][:, 0:uc], thrt[:, 1:2],
                                        None, ALU.is_lt)
                        v.scalar_tensor_tensor(qbuf[par][:, 0:uc], wbuf[par][:, 0:uc],
                                               thrt[:, 0:1], tbuf[:, 0:uc],
                                               ALU.is_gt, ALU.subtract)
                    K.em("v", qz)
                    K.inc("v", sqz)
                    K.wait("g", sqz)
                    def st(g, dst=dst, ch=ch, uc=uc, par=par):
                        g.dma_start(out=dst[ch * 128:(ch + 1) * 128, 0:uc],
                                    in_=qbuf[par][:, 0:uc]).then_inc(sst.h, 16)
                    K.em("g", st)
                    sst.n += 16
                if wi == 0:
                    K.wait("g", sqz)
                    def dbgq(g):
                        with nc.allow_non_contiguous_dma(reason="debug tap"):
                            g.dma_start(out=s_dbgq[:, 0:2], in_=thrt[:, :]).then_inc(sst.h, 16)
                            g.dma_start(out=s_dbgq[:, 2:3], in_=vecs[:, 136:137]).then_inc(sst.h, 16)
                            g.dma_start(out=s_dbgq[:, 3:4], in_=tbuf[:, 0:1]).then_inc(sst.h, 16)
                    K.em("g", dbgq)
                    sst.n += 48
            K.wait("g", sst)
            for (src, dst) in [(wr_T, s_wr), (wk_T, s_wk), (wv_T, s_wv)]:
                for ch in range(C8):
                    def c1(g, src=src, ch=ch):
                        g.dma_start(out=qbuf[0][:, 0:1024],
                                    in_=src[ch * 128:(ch + 1) * 128, :]).then_inc(sst.h, 16)
                    K.em("g", c1)
                    sst.n += 16
                    K.wait("g", sst)
                    def c2(g, dst=dst, ch=ch):
                        g.dma_start(out=dst[ch * 128:(ch + 1) * 128, :],
                                    in_=qbuf[0][:, 0:1024]).then_inc(sst.h, 16)
                    K.em("g", c2)
                    sst.n += 16
                    K.wait("g", sst)
            laT = ps.enter_context(nc.sbuf_tensor("wm_a", [LORA_R, 128], BF16))
            lbB = ps.enter_context(nc.sbuf_tensor("wm_b", [LORA_R, 3 * D], BF16))
            rev = ps.enter_context(nc.sbuf_tensor("wm_r", [128, 3 * D], BF16))
            psl = ps.enter_context(nc.psum_tensor("wm_p", [128, 512], F32))
            w1 = K.sem("wm_1"); w2 = K.sem("wm_2"); w3 = K.sem("wm_3"); w4 = K.sem("wm_4")
            w5 = K.sem("wm_5"); w6 = K.sem("wm_6")
            def ldb(g):
                g.dma_start(out=lbB[:, :], in_=la_b[:, :]).then_inc(w1.h, 16)
            K.em("g", ldb)
            w1.n += 16
            for mch in range(C8):
                def lda(g, mch=mch):
                    g.dma_start(out=laT[:, :], in_=la_aT[:, mch * 128:(mch + 1) * 128]) \
                     .then_inc(w1.h, 16)
                K.em("g", lda)
                w1.n += 16
                K.wait("t", w1)
                if mch > 0:
                    K.wait("t", w4, w4.n)
                for (o, w) in slc(3 * D):
                    K.wait("t", w3, w3.n)
                    def mm(t, o=o, w=w):
                        t.matmul(psl[:, 0:w], laT[:, :], lbB[:, o:o + w], start=True, stop=True)
                    K.em("t", mm)
                    K.inc("t", w2)
                    K.wait("v", w2)
                    def ev(v, o=o, w=w):
                        v.tensor_copy(rev[:, o:o + w], psl[:, 0:w])
                    K.em("v", ev)
                    K.inc("v", w3)
                K.wait("g", w3)
                def st(g, mch=mch):
                    g.dma_start(out=s_R[mch * 384:(mch + 1) * 384, :]
                                .rearrange("(p a) d -> p a d", p=128),
                                in_=rev[:, :].rearrange("p (a d) -> p a d", a=3)) \
                     .then_inc(w4.h, 16)
                K.em("g", st)
                w4.n += 16
            watf = dmy
            K.wait("y", w4)
            for ch in range(C8):
                if ch > 0:
                    K.wait("y", w6, w6.n)
                def ld(y, ch=ch):
                    y.dma_start_transpose(rev[:, :], s_R[:, ch * 128:(ch + 1) * 128]) \
                     .then_inc(w5.h, 16)
                    y.dma_start(out=watf[:, 0:3 * D],
                                in_=wattn_T[ch * 128:(ch + 1) * 128, :]).then_inc(w5.h, 16)
                K.em("y", ld)
                w5.n += 32
                K.wait("v", w5)
                def add(v, ch=ch):
                    v.tensor_add(rev[:, :], watf[:, 0:3 * D], rev[:, :])
                K.em("v", add)
                K.inc("v", w2)
                K.wait("g", w2)
                def st(g, ch=ch):
                    g.dma_start(out=s_wm[ch * 128:(ch + 1) * 128, :], in_=rev[:, :]) \
                     .then_inc(w6.h, 16)
                K.em("g", st)
                w6.n += 16
                K.wait("v", w6)
            K.barrier("ph0_done")

        _lnid = [0]

        def make_ln(ps):
            _lnid[0] += 1
            u = _lnid[0]
            h = {}
            h["px"] = ps.enter_context(nc.psum_tensor(f"ln{u}_px", [1, 512], F32))
            h["pq"] = ps.enter_context(nc.psum_tensor(f"ln{u}_pq", [1, 512], F32))
            h["pA"] = ps.enter_context(nc.psum_tensor(f"ln{u}_pA", [128, 512], F32))
            h["pB"] = ps.enter_context(nc.psum_tensor(f"ln{u}_pB", [128, 512], F32))
            h["sq"] = ps.enter_context(nc.sbuf_tensor(f"ln{u}_sq", [128, 512], F32))
            h["rw"] = ps.enter_context(nc.sbuf_tensor(f"ln{u}_rw", [1, 2048], F32))
            h["tm"] = ps.enter_context(nc.sbuf_tensor(f"ln{u}_tm", [128, 512], F32))
            h["ssq"] = K.sem("ln_ssq")
            h["smm"] = K.sem("ln_smm")
            h["srw"] = K.sem("ln_srw")
            h["sbc"] = K.sem("ln_sbc")
            h["sap"] = K.sem("ln_sap")
            return h

        def emit_ln(h, xs, W, hout, ca, sa, zout=None):
            for (o, w) in slc(W):
                for ch in range(C8):
                    xap, xo = xs[ch]
                    K.wait("s", h["smm"])
                    def sq(s, xap=xap, xo=xo, o=o, w=w):
                        s.activation(h["sq"][:, 0:w], xap[:, xo + o:xo + o + w], AF.Square)
                    K.em("s", sq)
                    K.inc("s", h["ssq"])
                    K.wait("t", h["ssq"])
                    if ch == 0:
                        K.wait("t", h["srw"])
                    def mm(t, xap=xap, xo=xo, o=o, w=w, ch=ch):
                        t.matmul(h["px"][0:1, 0:w], ones_c[:], xap[:, xo + o:xo + o + w],
                                 start=(ch == 0), stop=(ch == C8 - 1))
                        t.matmul(h["pq"][0:1, 0:w], ones_c[:], h["sq"][:, 0:w],
                                 start=(ch == 0), stop=(ch == C8 - 1))
                    K.em("t", mm)
                    K.inc("t", h["smm"])
                K.wait("v", h["smm"])
                def rows(v, w=w):
                    m = h["rw"][0:1, 0:w]
                    e2 = h["rw"][0:1, 512:512 + w]
                    A = h["rw"][0:1, 1024:1024 + w]
                    B = h["rw"][0:1, 1536:1536 + w]
                    v.tensor_scalar(m, h["px"][0:1, 0:w], 1.0 / D, None, ALU.mult)
                    v.tensor_scalar(e2, h["pq"][0:1, 0:w], 1.0 / D, None, ALU.mult)
                    v.tensor_mul(B, m, m)
                    v.tensor_sub(e2, e2, B)
                    v.tensor_scalar(e2, e2, EPS_LN, None, ALU.add)
                K.em("v", rows)
                K.inc("v", h["srw"])
                K.wait("s", h["srw"])
                def rsq(s, w=w):
                    s.activation(h["rw"][0:1, 1024:1024 + w], h["rw"][0:1, 512:512 + w], AF.Sqrt)
                K.em("s", rsq)
                K.inc("s", h["srw"])
                K.wait("v", h["srw"])
                def rows2(v, w=w):
                    m = h["rw"][0:1, 0:w]
                    A = h["rw"][0:1, 1024:1024 + w]
                    B = h["rw"][0:1, 1536:1536 + w]
                    v.reciprocal(A, A)
                    v.tensor_mul(B, m, A)
                    v.tensor_scalar(B, B, -1.0, None, ALU.mult)
                K.em("v", rows2)
                K.inc("v", h["srw"])
                K.wait("t", h["srw"])
                K.wait("t", h["sap"])
                def bc(t, w=w):
                    t.matmul(h["pA"][:, 0:w], ones_r[:], h["rw"][0:1, 1024:1024 + w],
                             start=True, stop=True)
                    t.matmul(h["pB"][:, 0:w], ones_r[:], h["rw"][0:1, 1536:1536 + w],
                             start=True, stop=True)
                K.em("t", bc)
                K.inc("t", h["sbc"])
                K.wait("v", h["sbc"])
                for ch in range(C8):
                    xap, xo = xs[ch]
                    hap, ho = hout[ch]
                    zap = zout[ch] if zout else None
                    def ap_(v, xap=xap, xo=xo, hap=hap, ho=ho, zap=zap, o=o, w=w, ch=ch):
                        tgt = zap[0][:, zap[1] + o:zap[1] + o + w] if zap else h["tm"][:, 0:w]
                        v.tensor_mul(tgt, xap[:, xo + o:xo + o + w], h["pA"][:, 0:w])
                        v.tensor_add(tgt, tgt, h["pB"][:, 0:w])
                        v.tensor_scalar(hap[:, ho + o:ho + o + w], tgt,
                                        mc(ca, ch), mc(sa, ch), ALU.mult, ALU.add)
                    K.em("v", ap_)
                K.inc("v", h["sap"])
                K.wait("t", h["sap"])

        _mmid = [0]

        class MM:
            def __init__(self, ps):
                _mmid[0] += 1
                u = _mmid[0]
                self.pb = [ps.enter_context(nc.psum_tensor(f"mm{u}p{j}", [128, 512], F32))
                           for j in range(2)]
                self.sm = [K.sem(f"mm_s{j}") for j in range(2)]
                self.se = [K.sem(f"mm_e{j}") for j in range(2)]
                self.uses = [0, 0]
                self.rr = 0

            def stage(self, n_out, W, nin, lhsT_fn, rhs_fn, epil, ep_eng="v",
                      extra_reader=None):
                for o in range(n_out):
                    for (off, w) in slc(W):
                        j = self.rr % 2
                        self.rr += 1
                        K.wait("t", self.se[j])
                        self.uses[j] += 1
                        psu = self.pb[j]
                        def mm(t, o=o, off=off, w=w, psu=psu, nin=nin):
                            for c in range(nin):
                                t.matmul(psu[:, 0:w], lhsT_fn(c, o), rhs_fn(c, off, w),
                                         start=(c == 0), stop=(c == nin - 1))
                        K.em("t", mm)
                        K.inc("t", self.sm[j])
                        K.wait(ep_eng, self.sm[j])
                        def ep(e, o=o, off=off, w=w, psu=psu):
                            epil(e, o, off, w, psu)
                        K.em(ep_eng, ep)
                        K.inc(ep_eng, self.se[j])
                        if extra_reader:
                            eng2, ep2 = extra_reader
                            K.wait(eng2, self.sm[j])
                            def e2(e, o=o, off=off, w=w, psu=psu):
                                ep2(e, o, off, w, psu)
                            K.em(eng2, e2)
                            K.inc(eng2, self.se[j])

            def drain(self, eng):
                for j in range(2):
                    K.wait(eng, self.se[j])

        with contextlib.ExitStack() as ps:
            xsp = [ps.enter_context(nc.sbuf_tensor(f"xsp{c}", [128, SW], F32))
                   for c in range(C8)]
            hb = [ps.enter_context(nc.sbuf_tensor(f"hb{c}", [128, SW], BF16))
                  for c in range(C8)]
            xib = [ps.enter_context(nc.sbuf_tensor(f"xib{c}", [128, SW], BF16))
                   for c in range(C16)]
            xcb = [ps.enter_context(nc.sbuf_tensor(f"xcb{c}", [128, SW], BF16))
                   for c in range(C16)]
            xrb = [ps.enter_context(nc.sbuf_tensor(f"xrb{c}", [128, SW], BF16))
                   for c in range(C8)]
            rb = [ps.enter_context(nc.sbuf_tensor(f"rb{c}", [128, SW], BF16))
                  for c in range(C8)]
            kab = [ps.enter_context(nc.sbuf_tensor(f"kab{c}", [128, SW], BF16))
                   for c in range(C8)]
            wp = [ps.enter_context(nc.sbuf_tensor(f"wp{c}", [128, 2048], BF16))
                  for c in range(C16)]
            ca1 = ps.enter_context(nc.sbuf_tensor("cva", [128, SW], F32))
            ca2 = ps.enter_context(nc.sbuf_tensor("cvb", [128, SW], F32))
            ca3 = ps.enter_context(nc.sbuf_tensor("cvc", [128, SW], F32))
            ept = ps.enter_context(nc.sbuf_tensor("ept", [128, 512], F32))
            lnh = make_ln(ps)
            mm = MM(ps)
            s_x = K.sem("p1x")
            s_w = K.sem("p1w")
            s_wdone = K.sem("p1wd")
            s_cv = K.sem("p1cv")
            s_cg = K.sem("p1cg")
            s_sil = K.sem("p1sil")
            s_y = K.sem("p1y")
            s_x2st = K.sem("p1x2")
            s_sp = K.sem("p1sp")

            wstage = [0]

            def loadw(src, nch, cols):
                if wstage[0] > 0:
                    K.wait("y", s_wdone, wstage[0])
                def ld(y, src=src, nch=nch, cols=cols):
                    for c in range(nch):
                        y.dma_start(out=wp[c][:, 0:cols],
                                    in_=src[c * 128:(c + 1) * 128, 0:cols]).then_inc(s_w.h, 16)
                K.em("y", ld)
                s_w.n += 16 * nch
                wstage[0] += 1
                K.wait("t", s_w)

            def z0(v):
                for c in range(C16):
                    v.memset(xcb[c][:, 0:4], 0.0)
            K.em("v", z0)
            s_z0 = K.sem("p1z0")
            K.inc("v", s_z0)
            K.wait("t", s_z0)

            for span in range(NSP):
                sb0 = span * SW
                if span > 0:
                    K.wait("y", s_x2st, span * 16 * C8)
                def ldx(y, sb0=sb0):
                    for c in range(C8):
                        y.dma_start(out=xsp[c][:, :],
                                    in_=xsh[c * 128:(c + 1) * 128, sb0:sb0 + SW]) \
                         .then_inc(s_x.h, 16)
                K.em("y", ldx)
                s_x.n += 16 * C8
                for e in ("v", "s", "t"):
                    K.wait(e, s_x)
                emit_ln(lnh, [(xsp[c], 0) for c in range(C8)], SW,
                        [(hb[c], 0) for c in range(C8)], "ca0", "sa0")
                loadw(s_wqin, C8, 2048)
                def ep_u(v, o, off, w, psu):
                    v.tensor_scalar(xib[o][:, off:off + w], psu[:, 0:w],
                                    vc("scl", 0), vc("b_in", o), ALU.mult, ALU.add)
                mm.stage(C16, SW, C8,
                         lambda c, o: wp[c][:, o * 128:(o + 1) * 128],
                         lambda c, off, w: hb[c][:, off:off + w], ep_u)
                K.inc("t", s_wdone)
                if span == 0:
                    mm.drain("v")
                    def hz(v):
                        for c in range(C16):
                            v.tensor_scalar(xib[c][:, 4:8], xib[c][:, 4:8],
                                            hflag[:, 0:1], None, ALU.mult)
                    K.em("v", hz)
                for c in range(C16):
                    mm.drain("v")
                    if c > 0 or span > 0:
                        K.wait("v", s_sil, s_sil.n)
                    def cva(v, c=c):
                        v.tensor_scalar(ca1[:, 3:SW], xib[c][:, 0:SW - 3],
                                        wcv_col(c, 0), None, ALU.mult)
                        v.scalar_tensor_tensor(ca1[:, 3:SW], xib[c][:, 1:SW - 2],
                                               wcv_col(c, 1), ca1[:, 3:SW], ALU.mult, ALU.add)
                        v.scalar_tensor_tensor(ca1[:, 3:SW], xib[c][:, 2:SW - 1],
                                               wcv_col(c, 2), ca1[:, 3:SW], ALU.mult, ALU.add)
                        v.scalar_tensor_tensor(ca1[:, 3:SW], xib[c][:, 3:SW],
                                               wcv_col(c, 3), ca1[:, 3:SW], ALU.mult, ALU.add)
                    K.em("v", cva)
                    K.inc("v", s_cv)
                    K.wait("s", s_cv)
                    def sil(s, c=c):
                        s.activation(xcb[c][:, 3:SW], ca1[:, 3:SW], AF.Silu,
                                     bias=vc("b_conv", c))
                    K.em("s", sil)
                    K.inc("s", s_sil)
                loadw(s_wqx, C16, 2048)
                K.wait("t", s_sil)
                def gbuf(o):
                    return xrb[o] if o < C8 else rb[o - C8]
                def ep_p(s, o, off, w, psu):
                    s.activation(gbuf(o)[:, off:off + w], psu[:, 0:w], AF.Sigmoid,
                                 bias=vc("b_x", o), scale=vc("scl", 1))
                mm.stage(C16, SW, C16,
                         lambda c, o: wp[c][:, o * 128:(o + 1) * 128],
                         lambda c, off, w: xcb[c][:, off:off + w], ep_p, ep_eng="s")
                K.wait("v", mm.sm[0], mm.sm[0].n)
                K.wait("v", mm.sm[1], mm.sm[1].n)
                mm.drain("v")
                def ymul(v):
                    for c in range(C16):
                        v.tensor_mul(xcb[c][:, :], xcb[c][:, :], gbuf(c)[:, :])
                K.em("v", ymul)
                K.inc("v", s_y)
                K.wait("t", s_y)
                K.inc("t", s_wdone)
                loadw(s_wqout, C16, 1024)
                def ep_o(v, o, off, w, psu):
                    v.tensor_scalar(ept[:, 0:w], psu[:, 0:w], vc("scl", 2), vc("b_out", o),
                                    ALU.mult, ALU.add)
                    v.scalar_tensor_tensor(xsp[o][:, off:off + w], ept[:, 0:w], mc("ga0", o),
                                           xsp[o][:, off:off + w], ALU.mult, ALU.add)
                mm.stage(C8, SW, C16,
                         lambda c, o: wp[c][:, o * 128:(o + 1) * 128],
                         lambda c, off, w: xcb[c][:, off:off + w], ep_o)
                K.inc("t", s_wdone)
                mm.drain("t")
                mm.drain("s")
                emit_ln(lnh, [(xsp[c], 0) for c in range(C8)], SW,
                        [(hb[c], 0) for c in range(C8)], "cb0", "sb0")
                if span == 0:
                    def hz2(v):
                        for c in range(C8):
                            v.tensor_scalar(hb[c][:, 7:8], hb[c][:, 7:8],
                                            hflag[:, 0:1], None, ALU.mult)
                    K.em("v", hz2)
                def shifts(v, mu, mum):
                    for c in range(C8):
                        v.tensor_scalar(ca1[:, 8:SW], hb[c][:, 8:SW], vc(mu, c),
                                        None, ALU.mult)
                        v.scalar_tensor_tensor(xrb[c][:, 8:SW], hb[c][:, 7:SW - 1],
                                               vc(mum, c), ca1[:, 8:SW], ALU.mult, ALU.add)
                K.em("v", lambda v: shifts(v, "mur", "murm"))
                K.inc("v", s_cv)
                loadw(s_wr, C8, 1024)
                K.wait("t", s_cv)
                def ep_r(s, o, off, w, psu):
                    s.activation(rb[o][:, 8 + off:8 + off + w], psu[:, 0:w], AF.Sigmoid)
                mm.stage(C8, SW - 8, C8,
                         lambda c, o: wp[c][:, o * 128:(o + 1) * 128],
                         lambda c, off, w: xrb[c][:, 8 + off:8 + off + w], ep_r, ep_eng="s")
                K.inc("t", s_wdone)
                mm.drain("v")
                mm.drain("t")
                K.em("v", lambda v: shifts(v, "muk", "mukm"))
                K.inc("v", s_cv)
                loadw(s_wk, C8, 1024)
                K.wait("t", s_cv)
                def ep_k(s, o, off, w, psu):
                    s.activation(kab[o][:, 8 + off:8 + off + w], psu[:, 0:w], AF.Relu)
                mm.stage(C8, SW - 8, C8,
                         lambda c, o: wp[c][:, o * 128:(o + 1) * 128],
                         lambda c, off, w: xrb[c][:, 8 + off:8 + off + w], ep_k, ep_eng="s")
                K.inc("t", s_wdone)
                mm.drain("v")
                sv_k = s_cv
                K.inc("s", sv_k)
                K.wait("v", sv_k)
                def ksq(v):
                    for c in range(C8):
                        v.tensor_mul(kab[c][:, 8:SW], kab[c][:, 8:SW], kab[c][:, 8:SW])
                K.em("v", ksq)
                K.inc("v", s_cv)
                loadw(s_wv, C8, 1024)
                K.wait("t", s_cv)
                def ep_v(v, o, off, w, psu):
                    v.tensor_mul(ept[:, 0:w], rb[o][:, 8 + off:8 + off + w], psu[:, 0:w])
                    v.scalar_tensor_tensor(xsp[o][:, 8 + off:8 + off + w], ept[:, 0:w],
                                           mc("gb0", o), xsp[o][:, 8 + off:8 + off + w],
                                           ALU.mult, ALU.add)
                mm.stage(C8, SW - 8, C8,
                         lambda c, o: wp[c][:, o * 128:(o + 1) * 128],
                         lambda c, off, w: kab[c][:, 8 + off:8 + off + w], ep_v)
                K.inc("t", s_wdone)
                mm.drain("y")
                K.inc("v", s_cv)
                K.wait("y", s_cv)
                def stx2(y, span=span):
                    for c in range(C8):
                        src = xsp[c][:, :].rearrange("p (a w) -> p a w", a=SPT)[:, :, 8:TW]
                        y.dma_start(out=s_x2[c * 128:(c + 1) * 128,
                                             span * SPT * 128:(span + 1) * SPT * 128]
                                    .rearrange("p (a w) -> p a w", a=SPT),
                                    in_=src).then_inc(s_x2st.h, 16)
                K.em("y", stx2)
                s_x2st.n += 16 * C8
                for e in ("t", "v", "s", "g"):
                    K.inc(e, s_sp)
                for e in ("t", "v", "s", "g"):
                    K.wait(e, s_sp)
                K.flush()
            K.barrier("ph1_done")

        QOFF = 24 * NSP2
        bnS = sbp("bnS", (128, 2 * QOFF))
        bnA = sbp("bnA", (128, 24))
        bnB = sbp("bnB", (128, 24))
        with contextlib.ExitStack() as ps:
            x2sp = [ps.enter_context(nc.sbuf_tensor(f"x2sp{c}", [128, 512], F32))
                    for c in range(C8)]
            h3sp = [ps.enter_context(nc.sbuf_tensor(f"h3sp{c}", [128, 512], BF16))
                    for c in range(C8)]
            wmP = [ps.enter_context(nc.sbuf_tensor(f"wmP{c}", [128, 3 * D], BF16))
                   for c in range(C8)]
            xasp = ps.enter_context(nc.sbuf_tensor("xasp", [128, 512], BF16))
            dum2 = ps.enter_context(nc.sbuf_tensor("dum2", [128, 512], F32))
            lnh = make_ln(ps)
            mm = MM(ps)
            s_x = K.sem("p2x")
            s_wl = K.sem("p2w")
            s_st = K.sem("p2st")
            def ldwm(y):
                for c in range(C8):
                    y.dma_start(out=wmP[c][:, :], in_=s_wm[c * 128:(c + 1) * 128, :]) \
                     .then_inc(s_wl.h, 16)
            K.em("y", ldwm)
            s_wl.n += 16 * C8
            K.wait("t", s_wl)
            for sp in range(NSP2):
                c0 = sp * 512
                if sp > 0:
                    K.wait("y", lnh["sap"])
                    K.wait("y", lnh["ssq"])
                    K.wait("y", lnh["smm"])
                def ldx(y, c0=c0):
                    for c in range(C8):
                        y.dma_start(out=x2sp[c][:, :],
                                    in_=s_x2[c * 128:(c + 1) * 128, c0:c0 + 512]) \
                         .then_inc(s_x.h, 16)
                K.em("y", ldx)
                s_x.n += 16 * C8
                for e in ("v", "s", "t"):
                    K.wait(e, s_x)
                emit_ln(lnh, [(x2sp[c], 0) for c in range(C8)], 512,
                        [(h3sp[c], 0) for c in range(C8)], "ca1", "sa1")
                for o in range(24):
                    j = mm.rr % 2
                    mm.rr += 1
                    K.wait("t", mm.se[j])
                    mm.uses[j] += 1
                    psu = mm.pb[j]
                    def qmm(t, o=o, psu=psu):
                        for c in range(C8):
                            t.matmul(psu[:, :], wmP[c][:, o * 128:(o + 1) * 128],
                                     h3sp[c][:, :], start=(c == 0), stop=(c == C8 - 1))
                    K.em("t", qmm)
                    K.inc("t", mm.sm[j])
                    K.wait("s", mm.sm[j])
                    if s_st.n > 0:
                        K.wait("s", s_st, s_st.n)
                    def qep(s, o=o, sp=sp, psu=psu):
                        s.activation(xasp[:, :], psu[:, :], AF.Copy,
                                     accum_out=bnS[:, o * NSP2 + sp:o * NSP2 + sp + 1])
                        s.activation(dum2[:, :], psu[:, :], AF.Square,
                                     accum_out=bnS[:, QOFF + o * NSP2 + sp:QOFF + o * NSP2 + sp + 1])
                    K.em("s", qep)
                    K.inc("s", mm.se[j])
                    K.wait("y", mm.se[j])
                    def qst(y, o=o, c0=c0):
                        y.dma_start(out=s_xa[o * 128:(o + 1) * 128, c0:c0 + 512],
                                    in_=xasp[:, :]).then_inc(s_st.h, 16)
                    K.em("y", qst)
                    s_st.n += 16
                K.flush()
            K.barrier("ph2_done")

        with contextlib.ExitStack() as ps:
            dumb = ps.enter_context(nc.sbuf_tensor("bn_d", [128, 24], F32))
            sr = K.sem("bn_r"); st_ = K.sem("bn_st"); scc = K.sem("bn_cc")
            sl2 = K.sem("bn_ld"); sab = K.sem("bn_ab")
            def red(v):
                for o in range(24):
                    if NSP2 == 1:
                        v.tensor_copy(bnA[:, o:o + 1], bnS[:, o:o + 1])
                        v.tensor_copy(bnB[:, o:o + 1], bnS[:, QOFF + o:QOFF + o + 1])
                    else:
                        v.reduce_sum(bnA[:, o:o + 1], bnS[:, o * NSP2:(o + 1) * NSP2], axis=AX)
                        v.reduce_sum(bnB[:, o:o + 1],
                                     bnS[:, QOFF + o * NSP2:QOFF + (o + 1) * NSP2], axis=AX)
            K.em("v", red)
            K.inc("v", sr)
            K.wait("g", sr)
            def bst(g):
                for o in range(24):
                    g.dma_start(out=row2col(cc_bn_in[0:1, o * 128:(o + 1) * 128]),
                                in_=bnA[:, o:o + 1]).then_inc(st_.h, 16)
                    g.dma_start(out=row2col(cc_bn_in[1:2, o * 128:(o + 1) * 128]),
                                in_=bnB[:, o:o + 1]).then_inc(st_.h, 16)
            K.em("g", bst)
            st_.n += 48 * 16
            K.wait("g", st_)
            def bcc(g):
                g.collective_compute("AllReduce", ALU.add,
                                     replica_groups=[list(range(NCORE))],
                                     ins=[cc_bn_in.opt()], outs=[cc_bn_out.opt()]) \
                 .then_inc(scc.h, 1)
                g.wait_ge(scc.h, 1)
            K.em("g", bcc)
            scc.n += 1
            K.wait("y", scc)
            def bld(y):
                for o in range(24):
                    y.dma_start(out=bnA[:, o:o + 1],
                                in_=row2col(cc_bn_out[0:1, o * 128:(o + 1) * 128])) \
                     .then_inc(sl2.h, 16)
                    y.dma_start(out=bnB[:, o:o + 1],
                                in_=row2col(cc_bn_out[1:2, o * 128:(o + 1) * 128])) \
                     .then_inc(sl2.h, 16)
            K.em("y", bld)
            sl2.n += 48 * 16
            K.wait("v", sl2)
            def ab1(v):
                v.tensor_scalar(bnA[:, :], bnA[:, :], 1.0 / S, None, ALU.mult)
                v.tensor_scalar(bnB[:, :], bnB[:, :], 1.0 / S, None, ALU.mult)
                v.tensor_mul(dumb[:, :], bnA[:, :], bnA[:, :])
                v.tensor_sub(bnB[:, :], bnB[:, :], dumb[:, :])
                v.tensor_scalar(bnB[:, :], bnB[:, :], EPS_LN, None, ALU.add)
            K.em("v", ab1)
            K.inc("v", sab)
            K.wait("s", sab)
            K.em("s", lambda s: s.activation(bnB[:, :], bnB[:, :], AF.Sqrt))
            K.inc("s", sab)
            K.wait("v", sab)
            def ab2(v):
                v.reciprocal(bnB[:, :], bnB[:, :])
                v.tensor_mul(bnB[:, :], bnB[:, :], vecs[:, VT["gma"]:VT["gma"] + 24])
                v.tensor_scalar(bnB[:, 0:8], bnB[:, 0:8], 1.0 / 32.0, None, ALU.mult)
                v.tensor_mul(bnA[:, :], bnA[:, :], bnB[:, :])
                v.tensor_copy(dumb[:, :], vecs[:, VT["gmb"]:VT["gmb"] + 24])
                v.tensor_scalar(dumb[:, 0:8], dumb[:, 0:8], 1.0 / 32.0, None, ALU.mult)
                v.tensor_sub(bnA[:, :], dumb[:, :], bnA[:, :])
            K.em("v", ab2)
            K.barrier("bn_done")

        ps34 = contextlib.ExitStack()
        qres = [ps34.enter_context(nc.sbuf_tensor(f"qres{c}", [128, SC], BF16))
                for c in range(C8)]
        with contextlib.ExitStack() as ps:
            xb = ps.enter_context(nc.sbuf_tensor("n_x", [128, 512], BF16))
            kv = ps.enter_context(nc.sbuf_tensor("n_kv", [128, 512], BF16))
            vt = ps.enter_context(nc.sbuf_tensor("n_vt", [128, 1024], BF16))
            ptr = ps.enter_context(nc.psum_tensor("n_tr", [128, 1024], BF16))
            s_x = K.sem("p3x"); s_n = K.sem("p3n"); s_t = K.sem("p3t")
            s_c = K.sem("p3c"); s_st = K.sem("p3st"); s_cc = K.sem("p3cc")
            for sp in range(NSP2):
                c0 = sp * 512
                for c in range(C8):
                    def ldq(y, c=c, c0=c0):
                        y.dma_start(out=xb[:, :], in_=s_xa[c * 128:(c + 1) * 128, c0:c0 + 512]) \
                         .then_inc(s_x.h, 16)
                    K.em("y", ldq)
                    s_x.n += 16
                    K.wait("v", s_x)
                    def nq(v, c=c, c0=c0):
                        v.tensor_scalar(qres[c][:, c0:c0 + 512], xb[:, :],
                                        bnB[:, c:c + 1], bnA[:, c:c + 1], ALU.mult, ALU.add)
                    K.em("v", nq)
                    K.inc("v", s_n)
                    K.wait("y", s_n)
                for c in range(C8):
                    def ldk(y, c=c, c0=c0):
                        y.dma_start(out=xb[:, :],
                                    in_=s_xa[(8 + c) * 128:(9 + c) * 128, c0:c0 + 512]) \
                         .then_inc(s_x.h, 16)
                    K.em("y", ldk)
                    s_x.n += 16
                    K.wait("v", s_x)
                    if s_st.n:
                        K.wait("v", s_st, s_st.n)
                    def nk(v, c=c):
                        v.tensor_scalar(kv[:, :], xb[:, :],
                                        bnB[:, 8 + c:9 + c], bnA[:, 8 + c:9 + c],
                                        ALU.mult, ALU.add)
                    K.em("v", nk)
                    K.inc("v", s_n)
                    K.wait("g", s_n)
                    def stk(g, c=c, c0=c0):
                        g.dma_start(out=cc_k_in[c * 128:(c + 1) * 128, c0:c0 + 512],
                                    in_=kv[:, :]).then_inc(s_st.h, 16)
                    K.em("g", stk)
                    s_st.n += 16
                    K.wait("y", s_st)
                for tt in range(4):
                    for c in range(C8):
                        def ldv(y, c=c, c0=c0, tt=tt):
                            y.dma_start(out=xb[:, 0:128],
                                        in_=s_xa[(16 + c) * 128:(17 + c) * 128,
                                                 c0 + tt * 128:c0 + (tt + 1) * 128]) \
                             .then_inc(s_x.h, 16)
                        K.em("y", ldv)
                        s_x.n += 16
                        K.wait("v", s_x)
                        if s_st.n:
                            K.wait("v", s_st, s_st.n)
                        def nv(v, c=c):
                            v.tensor_scalar(kv[:, 0:128], xb[:, 0:128],
                                            bnB[:, 16 + c:17 + c], bnA[:, 16 + c:17 + c],
                                            ALU.mult, ALU.add)
                        K.em("v", nv)
                        K.inc("v", s_n)
                        K.wait("t", s_n)
                        K.wait("t", s_c, s_c.n)
                        def tr(t, c=c):
                            t.transpose(ptr[:, c * 128:(c + 1) * 128], kv[:, 0:128], idbf[:])
                        K.em("t", tr)
                        K.inc("t", s_t)
                        K.wait("v", s_t)
                        def cp(v, c=c):
                            v.tensor_copy(vt[:, c * 128:(c + 1) * 128],
                                          ptr[:, c * 128:(c + 1) * 128])
                        K.em("v", cp)
                        K.inc("v", s_c)
                        K.wait("y", s_c)
                    K.wait("g", s_c)
                    def stv(g, c0=c0, tt=tt):
                        g.dma_start(out=ccvi[c0 + tt * 128:c0 + (tt + 1) * 128, :],
                                    in_=vt[:, :]).then_inc(s_st.h, 16)
                    K.em("g", stv)
                    s_st.n += 16
                    K.wait("v", s_st)
                K.flush()
            K.wait("g", s_st)
            def gath(g):
                g.collective_compute("AllGather", ALU.bypass,
                                     replica_groups=[list(range(NCORE))],
                                     ins=[cc_k_in.opt()], outs=[cc_k_out.opt()]) \
                 .then_inc(s_cc.h, 1)
                g.wait_ge(s_cc.h, 1)
                g.collective_compute("AllGather", ALU.bypass,
                                     replica_groups=[list(range(NCORE))],
                                     ins=[ccvi.opt()], outs=[ccvo.opt()]) \
                 .then_inc(s_cc.h, 1)
                g.wait_ge(s_cc.h, 2)
            K.em("g", gath)
            s_cc.n += 2
            K.barrier("gather_done")

        with contextlib.ExitStack() as ps:
            otbuf = ps.enter_context(nc.sbuf_tensor("otbuf", [128, 1024], BF16))
            kb_ = [ps.enter_context(nc.sbuf_tensor(f"kb{j}", [128, C8 * 1024], BF16))
                   for j in range(2)]
            vb_ = [ps.enter_context(nc.sbuf_tensor(f"vb{j}", [128, C8 * 1024], BF16))
                   for j in range(2)]
            Oacc = [ps.enter_context(nc.sbuf_tensor(f"Oa{m}", [128, 1024], F32))
                    for m in range(TPC)]
            Pb = [ps.enter_context(nc.sbuf_tensor(f"Pb{j}", [128, 1024], BF16))
                  for j in range(2)]
            Pt = [ps.enter_context(nc.sbuf_tensor(f"Pt{j}", [128, 1024], BF16))
                  for j in range(2)]
            mst = ps.enter_context(nc.sbuf_tensor("mst", [128, TPC], F32))
            lst = ps.enter_context(nc.sbuf_tensor("lst", [128, TPC], F32))
            t4 = [ps.enter_context(nc.sbuf_tensor(f"t4{j}", [128, 4], F32))
                  for j in range(2)]
            ps_s = [ps.enter_context(nc.psum_tensor(f"aps{j}", [128, 1024], F32))
                    for j in range(2)]
            ps_pv = ps.enter_context(nc.psum_tensor("appv", [128, 1024], F32))
            ps_pt = ps.enter_context(nc.psum_tensor("appt", [128, 1024], BF16))
            s_kv = K.sem("a_kv")
            s_kvf = K.sem("a_kvf")
            s_s = [K.sem(f"a_s{j}") for j in range(2)]
            s_sr = [K.sem(f"a_sr{j}") for j in range(2)]
            s_m = K.sem("a_m")
            s_e = [K.sem(f"a_e{j}") for j in range(2)]
            s_tr = [K.sem(f"a_tr{j}") for j in range(2)]
            s_tc = [K.sem(f"a_tc{j}") for j in range(2)]
            s_av = [K.sem(f"a_av{j}") for j in range(2)]
            s_pv = K.sem("a_pv")
            s_o = K.sem("a_o")
            def initO(v):
                for m in range(TPC):
                    v.memset(Oacc[m][:], 0.0)
                v.memset(mst[:], NEG)
                v.memset(lst[:], 0.0)
            K.em("v", initO)
            K.inc("v", s_o)
            K.wait("t", s_o)

            npair = [0]
            use_s = [0, 0]
            use_pb = [0, 0]
            use_pt = [0, 0]
            use_pv = [0]
            use_t4 = [0, 0]

            def attn_scores(m, b, buf, diag, blk_loads):
                if ASTAGE < 1:
                    return None
                par = npair[0] % 2
                npair[0] += 1
                K.wait("t", s_kv, blk_loads * 16)
                if use_s[par] > 0:
                    K.wait("t", s_sr[par], use_s[par] * (2 if ASTAGE >= 3 else 1))
                use_s[par] += 1
                def smm(t, m=m, buf=buf, par=par):
                    for c in range(C8):
                        for hh in range(2):
                            t.matmul(ps_s[par][:, hh * 512:(hh + 1) * 512],
                                     qres[c][:, m * 128:(m + 1) * 128],
                                     kb_[buf][:, c * 1024 + hh * 512:c * 1024 + (hh + 1) * 512],
                                     start=(c == 0), stop=(c == C8 - 1))
                K.em("t", smm)
                K.inc("t", s_s[par])
                if ASTAGE < 2:
                    K.em("t", lambda t, par=par: None)
                    K.wait("v", s_s[par])
                    def dummy_rd(v, par=par):
                        v.tensor_copy(t4[par][:, 0:1], ps_s[par][:, 0:1])
                    K.em("v", dummy_rd)
                    K.inc("v", s_sr[par])
                    return None
                K.wait("v", s_s[par])
                if use_t4[par] > 0:
                    K.wait("v", s_e[par], use_t4[par])
                use_t4[par] += 1
                def stt(v, m=m, par=par, diag=diag):
                    if diag:
                        v.tensor_add(ps_s[par][:, :], ps_s[par][:, :], trim[:, :])
                    v.reduce_max(t4[par][:, 0:1], ps_s[par][:, :], axis=AX)
                    v.tensor_max(t4[par][:, 1:2], t4[par][:, 0:1], mst[:, m:m + 1])
                    v.tensor_scalar(t4[par][:, 2:3], t4[par][:, 1:2], -1.0, None, ALU.mult)
                K.em("v", stt)
                K.inc("v", s_m)
                K.inc("v", s_sr[par])
                if ASTAGE < 3:
                    return None
                K.wait("s", s_m)
                if use_pb[par] > 0 and ASTAGE >= 4:
                    K.wait("s", s_tr[par], use_pb[par])
                use_pb[par] += 1
                def exx(s, m=m, par=par):
                    s.activation(t4[par][:, 3:4], mst[:, m:m + 1], AF.Exp,
                                 bias=t4[par][:, 2:3])
                    s.activation(Pb[par][:, :], ps_s[par][:, :], AF.Exp,
                                 bias=t4[par][:, 2:3], accum_out=t4[par][:, 0:1])
                K.em("s", exx)
                K.inc("s", s_e[par])
                K.inc("s", s_sr[par])
                K.wait("v", s_e[par])
                def lmu(v, m=m, par=par):
                    v.scalar_tensor_tensor(lst[:, m:m + 1], lst[:, m:m + 1], t4[par][:, 3:4],
                                           t4[par][:, 0:1], ALU.mult, ALU.add)
                    v.tensor_copy(mst[:, m:m + 1], t4[par][:, 1:2])
                K.em("v", lmu)
                K.inc("v", s_m)
                return (par, m, buf)

            def attn_pv(par, m, buf, blast):
                if ASTAGE < 4:
                    if blast:
                        K.inc("t", s_kvf)
                    return
                K.wait("t", s_e[par])
                if use_pt[par] > 0 and ASTAGE >= 5:
                    K.wait("t", s_av[par], use_pt[par])
                K.wait("t", s_tc[0], s_tc[0].n)
                K.wait("t", s_tc[1], s_tc[1].n)
                use_pt[par] += 1
                def trr(t, par=par):
                    for jj in range(8):
                        t.transpose(ps_pt[:, jj * 128:(jj + 1) * 128],
                                    Pb[par][:, jj * 128:(jj + 1) * 128], idbf[:])
                K.em("t", trr)
                K.inc("t", s_tr[par])
                K.wait("s", s_tr[par])
                def tc1(s, par=par):
                    s.copy(Pt[par][:, :], ps_pt[:, :])
                K.em("s", tc1)
                K.inc("s", s_tc[par])
                K.inc("s", s_tc[par])
                if ASTAGE < 5:
                    if blast:
                        K.inc("t", s_kvf)
                    return
                K.wait("t", s_tc[par])
                if use_pv[0] > 0:
                    K.wait("t", s_o, use_pv[0])
                use_pv[0] += 1
                def avv(t, buf=buf, par=par):
                    for jj in range(8):
                        for hh in range(2):
                            t.matmul(ps_pv[:, hh * 512:(hh + 1) * 512],
                                     Pt[par][:, jj * 128:(jj + 1) * 128],
                                     vb_[buf][:, jj * 1024 + hh * 512:jj * 1024 + (hh + 1) * 512],
                                     start=(jj == 0), stop=(jj == 7))
                K.em("t", avv)
                K.inc("t", s_pv)
                K.inc("t", s_av[par])
                K.wait("v", s_pv)
                def ouu(v, m=m, par=par):
                    v.scalar_tensor_tensor(Oacc[m][:], Oacc[m][:], t4[par][:, 3:4],
                                           ps_pv[:, :], ALU.mult, ALU.add)
                K.em("v", ouu)
                K.inc("v", s_o)
                if blast:
                    K.inc("t", s_kvf)

            nload = [0]
            pend = [None]
            for b in range(TPC):
                buf = b % 2
                if b >= 2:
                    K.wait("y", s_kvf, b - 1)
                def ldb(y, b=b, buf=buf):
                    for jj in range(8):
                        y.dma_start(out=kb_[buf][:, :].rearrange("p (c n) -> p c n", c=C8)
                                    [:, :, jj * 128:(jj + 1) * 128],
                                    in_=cc_k_out[jj * D:(jj + 1) * D, b * 128:(b + 1) * 128]
                                    .rearrange("(c p) n -> c p n", p=128).transpose([1, 0, 2])) \
                         .then_inc(s_kv.h, 16)
                        y.dma_start(out=vb_[buf][:, jj * 1024:(jj + 1) * 1024],
                                    in_=ccvo[jj * SC + b * 128:jj * SC + (b + 1) * 128, :]) \
                         .then_inc(s_kv.h, 16)
                K.em("y", ldb)
                s_kv.n += 16 * 16
                nload[0] += 16
                for m in range(b, TPC):
                    blast = (m == TPC - 1)
                    r = attn_scores(m, b, buf, m == b, nload[0])
                    if pend[0] is not None:
                        attn_pv(*pend[0])
                    if r is not None:
                        pend[0] = (r[0], r[1], r[2], blast)
                    else:
                        pend[0] = None
                        if blast:
                            K.inc("t", s_kvf)
                K.flush()
            if pend[0] is not None:
                attn_pv(*pend[0])
                K.flush()
            s_n2 = K.sem("a_n2"); s_t2 = K.sem("a_t2"); s_c2 = K.sem("a_c2")
            s_o2 = K.sem("a_o2")
            for m in range(TPC):
                K.wait("v", s_o)
                K.wait("v", s_c2, s_c2.n)
                K.wait("v", s_tr[0], s_tr[0].n)
                K.wait("v", s_tr[1], s_tr[1].n)
                def onr(v, m=m):
                    v.reciprocal(t4[0][:, 0:1], lst[:, m:m + 1])
                    v.tensor_scalar(Pb[0][:, :], Oacc[m][:], t4[0][:, 0:1], None, ALU.mult)
                K.em("v", onr)
                K.inc("v", s_n2)
                K.wait("t", s_n2)
                K.wait("t", s_c2, s_c2.n)
                K.wait("t", s_tc[0], s_tc[0].n)
                K.wait("t", s_tc[1], s_tc[1].n)
                def otr(t):
                    for c in range(C8):
                        t.transpose(ps_pt[:, c * 128:(c + 1) * 128],
                                    Pb[0][:, c * 128:(c + 1) * 128], idbf[:])
                K.em("t", otr)
                K.inc("t", s_t2)
                K.wait("v", s_t2)
                K.wait("v", s_o2, s_o2.n)
                def ocp(v, m=m):
                    v.tensor_copy(otbuf[:, :], ps_pt[:, :])
                K.em("v", ocp)
                K.inc("v", s_c2)
                K.wait("y", s_c2)
                def ost(y, m=m):
                    y.dma_start(out=s_ot[:, m * 128:(m + 1) * 128]
                                .rearrange("(c p) n -> c p n", p=128).transpose([1, 0, 2]),
                                in_=otbuf[:, :].rearrange("p (c n) -> p c n", c=C8)) \
                     .then_inc(s_o2.h, 16)
                K.em("y", ost)
                s_o2.n += 16
            K.barrier("attn_done")
        ps34.close()

        with contextlib.ExitStack() as ps:
            lbB = ps.enter_context(nc.sbuf_tensor("mtb", [LORA_R, D], BF16))
            laT = ps.enter_context(nc.sbuf_tensor("mta", [LORA_R, D], BF16))
            wpT = [ps.enter_context(nc.sbuf_tensor(f"mtt{c}", [128, D], F32))
                   for c in range(C8)]
            wmp = [ps.enter_context(nc.sbuf_tensor(f"mtm{c}", [128, D], BF16))
                   for c in range(C8)]
            MT = [ps.enter_context(nc.sbuf_tensor(f"mtn{c}", [128, D], BF16))
                  for c in range(C8)]
            x3sp = [ps.enter_context(nc.sbuf_tensor(f"x3sp{c}", [128, 512], F32))
                    for c in range(C8)]
            otsp = [ps.enter_context(nc.sbuf_tensor(f"otsp{c}", [128, 512], BF16))
                    for c in range(C8)]
            psl = ps.enter_context(nc.psum_tensor("mtp", [128, 512], F32))
            mm = MM(ps)
            m1 = K.sem("mt1"); m2 = K.sem("mt2"); m3 = K.sem("mt3"); m1b = K.sem("mt1b")
            s_x = K.sem("pj_x"); s_st = K.sem("pj_st")
            def ldm(g):
                g.dma_start(out=lbB[:, :], in_=lp_b[:, :]).then_inc(m1.h, 16)
                g.dma_start(out=laT[:, :], in_=lp_aT[:, :]).then_inc(m1.h, 16)
                for c in range(C8):
                    g.dma_start(out=MT[c][:, :], in_=wproj_nat[c * 128:(c + 1) * 128, :]) \
                     .then_inc(m1.h, 16)
            K.em("g", ldm)
            m1.n += 16 * 10
            def ldm2(y):
                for c in range(C8):
                    y.dma_start(out=wpT[c][:, :], in_=wproj_T[c * 128:(c + 1) * 128, :]) \
                     .then_inc(m1b.h, 16)
            K.em("y", ldm2)
            m1b.n += 16 * 8
            K.wait("t", m1)
            K.wait("t", m1b)
            for n in range(C8):
                for (o, w) in slc(D):
                    K.wait("t", m3, m3.n)
                    def mmi(t, n=n, o=o, w=w):
                        t.matmul(psl[:, 0:w], lbB[:, n * 128:(n + 1) * 128],
                                 laT[:, o:o + w], start=True, stop=True)
                    K.em("t", mmi)
                    K.inc("t", m2)
                    K.wait("v", m2)
                    def evi(v, n=n, o=o, w=w):
                        v.tensor_add(wmp[n][:, o:o + w], psl[:, 0:w], wpT[n][:, o:o + w])
                    K.em("v", evi)
                    K.inc("v", m3)
            for c in range(C8):
                for (o, w) in slc(D):
                    K.wait("t", m3, m3.n)
                    def mmj(t, c=c, o=o, w=w):
                        for k in range(C8):
                            t.matmul(psl[:, 0:w], MT[k][:, c * 128:(c + 1) * 128],
                                     wmp[k][:, o:o + w], start=(k == 0), stop=(k == C8 - 1))
                    K.em("t", mmj)
                    K.inc("t", m2)
                    K.wait("v", m2)
                    def evj(v, c=c, o=o, w=w):
                        v.tensor_copy(wpT[c][:, o:o + w], psl[:, 0:w])
                    K.em("v", evj)
                    K.inc("v", m3)
            K.wait("v", m3)
            def cstm(v):
                for c in range(C8):
                    v.tensor_copy(MT[c][:, :], wpT[c][:, :])
            K.em("v", cstm)
            K.inc("v", m3)
            K.wait("t", m3)
            for sp in range(NSP2):
                c0 = sp * 512
                if sp > 0:
                    K.wait("y", s_st, sp * 16 * C8)
                def ldx(y, c0=c0):
                    for c in range(C8):
                        y.dma_start(out=x3sp[c][:, :],
                                    in_=s_x2[c * 128:(c + 1) * 128, c0:c0 + 512]) \
                         .then_inc(s_x.h, 16)
                        y.dma_start(out=otsp[c][:, :],
                                    in_=s_ot[c * 128:(c + 1) * 128, c0:c0 + 512]) \
                         .then_inc(s_x.h, 16)
                K.em("y", ldx)
                s_x.n += 32 * C8
                K.wait("v", s_x)
                K.wait("t", s_x)
                def ep_pj(v, o, off, w, psu, c0=c0):
                    v.scalar_tensor_tensor(x3sp[o][:, 0:w], psu[:, 0:w], mc("ga1", o),
                                           x3sp[o][:, 0:w], ALU.mult, ALU.add)
                mm.stage(C8, 512, C8,
                         lambda c, o: MT[c][:, o * 128:(o + 1) * 128],
                         lambda c, off, w: otsp[c][:, off:off + w], ep_pj)
                mm.drain("y")
                def stx(y, c0=c0):
                    for c in range(C8):
                        y.dma_start(out=s_x3[c * 128:(c + 1) * 128, c0:c0 + 512],
                                    in_=x3sp[c][:, :]).then_inc(s_st.h, 16)
                K.em("y", stx)
                s_st.n += 16 * C8
                K.flush()
            K.barrier("proj_done")

        h4res = [sbp(f"h4res{c}", (128, SC), BF16) for c in range(C8)]
        cwrow = sbp("cwrow", (NEXP, SC))
        with contextlib.ExitStack() as ps:
            x3sp = [ps.enter_context(nc.sbuf_tensor(f"g_x{c}", [128, 512], F32))
                    for c in range(C8)]
            zsp = [ps.enter_context(nc.sbuf_tensor(f"g_z{c}", [128, 512], F32))
                   for c in range(C8)]
            gwsb = [ps.enter_context(nc.sbuf_tensor(f"g_w{c}", [128, NEXP], F32))
                    for c in range(C8)]
            dsb = ps.enter_context(nc.sbuf_tensor("g_d", [NEXP, 1], F32))
            lrow = ps.enter_context(nc.sbuf_tensor("g_l", [NEXP, 512], F32))
            tokl = ps.enter_context(nc.sbuf_tensor("g_tl", [128, NEXP], F32))
            tokg = ps.enter_context(nc.sbuf_tensor("g_tg", [128, NEXP], F32))
            tokm = ps.enter_context(nc.sbuf_tensor("g_tm", [128, NEXP], F32))
            t8 = ps.enter_context(nc.sbuf_tensor("g_t8", [128, 8], F32))
            plg = ps.enter_context(nc.psum_tensor("g_pl", [NEXP, 512], F32))
            plt = ps.enter_context(nc.psum_tensor("g_pt", [128, NEXP], F32))
            lnh = make_ln(ps)
            sg = [K.sem(f"g_{k}") for k in range(8)]
            s_x = K.sem("g_x")
            def ldgw(y):
                for c in range(C8):
                    y.dma_start(out=gwsb[c][:, :], in_=gwi[c * 128:(c + 1) * 128, :]) \
                     .then_inc(sg[0].h, 16)
            K.em("y", ldgw)
            sg[0].n += 16 * C8
            K.wait("t", sg[0])
            def dmm(t):
                for c in range(C8):
                    t.matmul(plg[0:NEXP, 0:1], gwsb[c][:, :], mc("sb1", c),
                             start=(c == 0), stop=(c == C8 - 1))
            K.em("t", dmm)
            K.inc("t", sg[1])
            K.wait("v", sg[1])
            def dcp(v):
                v.tensor_copy(dsb[:, :], plg[0:NEXP, 0:1])
                for c in range(C8):
                    v.tensor_scalar(gwsb[c][:, :], gwsb[c][:, :], mc("cb1", c), None, ALU.mult)
            K.em("v", dcp)
            K.inc("v", sg[2])
            K.wait("t", sg[2])
            for sp in range(NSP2):
                c0 = sp * 512
                if sp > 0:
                    K.wait("y", lnh["sap"])
                    K.wait("y", lnh["ssq"])
                    K.wait("y", lnh["smm"])
                    K.wait("y", sg[3])
                def ldx(y, c0=c0):
                    for c in range(C8):
                        y.dma_start(out=x3sp[c][:, :],
                                    in_=s_x3[c * 128:(c + 1) * 128, c0:c0 + 512]) \
                         .then_inc(s_x.h, 16)
                K.em("y", ldx)
                s_x.n += 16 * C8
                for e in ("v", "s", "t"):
                    K.wait(e, s_x)
                emit_ln(lnh, [(x3sp[c], 0) for c in range(C8)], 512,
                        [(h4res[c], c0) for c in range(C8)], "cb1", "sb1",
                        zout=[(zsp[c], 0) for c in range(C8)])
                K.wait("t", lnh["sap"])
                K.wait("t", sg[6], sg[6].n)
                def lmm(t):
                    for c in range(C8):
                        t.matmul(plg[:, :], gwsb[c][:, :], zsp[c][:, :],
                                 start=(c == 0), stop=(c == C8 - 1))
                K.em("t", lmm)
                K.inc("t", sg[3])
                K.wait("v", sg[3])
                def lep(v):
                    v.tensor_scalar(lrow[:, :], plg[:, :], dsb[0:NEXP, 0:1], None, ALU.add)
                K.em("v", lep)
                K.inc("v", sg[4])
                for tt in range(4):
                    K.wait("t", sg[4])
                    K.wait("t", sg[6], sg[6].n)
                    def ltr(t, tt=tt):
                        t.transpose(plt[:, :], lrow[:, tt * 128:(tt + 1) * 128], idf[0:8, 0:8])
                    K.em("t", ltr)
                    K.inc("t", sg[5])
                    K.wait("v", sg[5])
                    def tp1(v):
                        v.tensor_copy(tokl[:, :], plt[:, :])
                        v.reduce_max(t8[:, 0:1], tokl[:, :], axis=AX)
                        v.tensor_scalar(t8[:, 1:2], t8[:, 0:1], -1.0, None, ALU.mult)
                    K.em("v", tp1)
                    K.inc("v", sg[6])
                    K.wait("s", sg[6])
                    def tp2(s):
                        s.activation(tokg[:, :], tokl[:, :], AF.Exp, bias=t8[:, 1:2],
                                     accum_out=t8[:, 2:3])
                    K.em("s", tp2)
                    K.inc("s", sg[7])
                    K.wait("v", sg[7])
                    def tp3(v):
                        v.reciprocal(t8[:, 3:4], t8[:, 2:3])
                        v.tensor_scalar(tokg[:, :], tokg[:, :], t8[:, 3:4], None, ALU.mult)
                        v.max(out=tokm[:, :], in_=tokg[:, :])
                        v.tensor_add(t8[:, 2:3], tokm[:, 0:1], tokm[:, 1:2])
                        v.tensor_scalar(t8[:, 2:3], t8[:, 2:3], 1e-8, None, ALU.add)
                        v.reciprocal(t8[:, 2:3], t8[:, 2:3])
                        v.tensor_mul(t8[:, 4:5], tokm[:, 0:1], t8[:, 2:3])
                        v.tensor_mul(t8[:, 5:6], tokm[:, 1:2], t8[:, 2:3])
                        v.tensor_scalar(tokl[:, :], tokg[:, :], tokm[:, 0:1], t8[:, 4:5],
                                        ALU.is_equal, ALU.mult)
                        v.tensor_scalar(tokg[:, :], tokg[:, :], tokm[:, 1:2], t8[:, 5:6],
                                        ALU.is_equal, ALU.mult)
                        v.tensor_add(tokl[:, :], tokl[:, :], tokg[:, :])
                    K.em("v", tp3)
                    K.inc("v", sg[6])
                    K.wait("t", sg[6])
                    def ctr(t, tt=tt):
                        t.transpose(plg[0:NEXP, 0:128], tokl[:, :], idf[:])
                    K.em("t", ctr)
                    K.inc("t", sg[5])
                    K.wait("v", sg[5])
                    def ccp(v, tt=tt, c0=c0):
                        v.tensor_copy(cwrow[:, c0 + tt * 128:c0 + (tt + 1) * 128],
                                      plg[0:NEXP, 0:128])
                    K.em("v", ccp)
                    K.inc("v", sg[6])
                K.flush()
            K.barrier("gate_done")

        with contextlib.ExitStack() as ps:
            w1b = [ps.enter_context(nc.sbuf_tensor(f"e_w1{c}", [128, 1024], BF16))
                   for c in range(C8)]
            w2b = [ps.enter_context(nc.sbuf_tensor(f"e_w2{c}", [128, 1024], BF16))
                   for c in range(C8)]
            hebf = [ps.enter_context(nc.sbuf_tensor(f"e_h{c}", [128, SC], BF16))
                    for c in range(C8)]
            cwbf = ps.enter_context(nc.sbuf_tensor("e_cw", [128, SC], F32))
            tsb = [ps.enter_context(nc.sbuf_tensor(f"e_t{j}", [128, SC], F32))
                   for j in range(2)]
            tmp4 = ps.enter_context(nc.sbuf_tensor("e_tm", [128, 512], F32))
            b2sb = ps.enter_context(nc.sbuf_tensor("e_b2", [128, NEXP * C8], F32))
            selb = ps.enter_context(nc.sbuf_tensor("e_sel", [NEXP, 128], F32))
            ridx = ps.enter_context(nc.sbuf_tensor("e_ri", [NEXP, 128], F32))
            pb8 = [ps.enter_context(nc.psum_tensor(f"e_p{j}", [128, 512], F32))
                   for j in range(8)]
            pcw = pb8[0]
            sA = K.sem("e_a"); sB = K.sem("e_b"); sC = K.sem("e_c")
            sD = K.sem("e_d"); sE = K.sem("e_e")
            sW1 = K.sem("e_w1s"); sW2 = K.sem("e_w2s")
            sm8 = [K.sem(n) for n in ("mm_s0", "mm_s1", "a_s0", "a_s1",
                                      "a_e0", "a_e1", "a_tr0", "a_tr1")]
            se8 = [K.sem(n) for n in ("mm_e0", "mm_e1", "a_sr0", "a_sr1",
                                      "a_tc0", "a_tc1", "a_av0", "a_av1")]
            gct = [0]
            def ldb2(g):
                g.dma_start(out=b2sb[:, :], in_=b2c[:, :]).then_inc(sA.h, 16)
                g.dma_start(out=ridx[:, :], in_=rowidx8[:, :]).then_inc(sA.h, 16)
            K.em("g", ldb2)
            sA.n += 32

            def stage4(n_out, nin, lhsT_fn, rhs_fn, epil, ep_eng):
                for o in range(n_out):
                    g0 = (gct[0] % 2) * 4
                    gct[0] += 1
                    for j in range(NSP2):
                        K.wait("t", se8[g0 + j])
                    def mmo(t, o=o, nin=nin, g0=g0):
                        for c in range(nin):
                            lh = lhsT_fn(c, o)
                            for j in range(NSP2):
                                t.matmul(pb8[g0 + j][:, :], lh, rhs_fn(c, j),
                                         start=(c == 0), stop=(c == nin - 1))
                    K.em("t", mmo)
                    for j in range(NSP2):
                        K.inc("t", sm8[g0 + j])
                    for j in range(NSP2):
                        K.wait(ep_eng, sm8[g0 + j])
                        def ep(e, o=o, j=j, g0=g0):
                            epil(e, o, j, pb8[g0 + j])
                        K.em(ep_eng, ep)
                        K.inc(ep_eng, se8[g0 + j])

            nacc = [0]
            for e in range(NEXP):
                K.wait("v", sE, sE.n)
                K.wait("v", sA, 32)
                def setsel(v, e=e):
                    v.tensor_scalar(selb[:, :], ridx[:, :], float(e), None, ALU.is_equal)
                K.em("v", setsel)
                K.inc("v", sB)
                for sp in range(NSP2):
                    K.wait("t", sB, sB.n)
                    K.wait("t", sD, sD.n)
                    K.wait("t", se8[0])
                    def cbc(t, e=e, sp=sp):
                        t.matmul(pcw[:, :], selb[:, :], cwrow[0:NEXP, sp * 512:(sp + 1) * 512],
                                 start=True, stop=True)
                    K.em("t", cbc)
                    K.inc("t", sE)
                    K.wait("v", sE)
                    def cbv(v, sp=sp):
                        v.tensor_copy(cwbf[:, sp * 512:(sp + 1) * 512], pcw[:, :])
                    K.em("v", cbv)
                    K.inc("v", sD)
                    K.inc("v", se8[0])
                K.wait("t", sD)
                for h in range(2):
                    for j in range(8):
                        K.wait("g", se8[j])
                    def ldw(g, e=e, h=h):
                        for c in range(C8):
                            g.dma_start(out=w1b[c][:, :],
                                        in_=w1e[e, c * 128:(c + 1) * 128,
                                                h * 1024:(h + 1) * 1024]).then_inc(sW1.h, 16)
                        for c in range(C8):
                            g.dma_start(out=w2b[c][:, :],
                                        in_=w2e[e, h * 1024 + c * 128:h * 1024 + (c + 1) * 128,
                                                :]).then_inc(sW2.h, 16)
                    K.em("g", ldw)
                    sW1.n += 16 * C8
                    sW2.n += 16 * C8
                    K.wait("t", sW1)
                    def ep_h(s, o, j, psu, e=e, h=h):
                        s.activation(hebf[o][:, j * 512:(j + 1) * 512], psu[:, :], AF.Relu,
                                     bias=b1sb[:, e * C16 + h * 8 + o:e * C16 + h * 8 + o + 1])
                    stage4(C8, C8,
                           lambda c, o: w1b[c][:, o * 128:(o + 1) * 128],
                           lambda c, j: h4res[c][:, j * 512:(j + 1) * 512],
                           ep_h, "s")
                    K.wait("t", sW2)
                    for j in range(8):
                        K.wait("t", se8[j])
                    def ep_o2(v, o, j, psu, par, e=e, h=h):
                        d = tsb[par][:, j * 512:(j + 1) * 512]
                        if h == 0:
                            v.tensor_scalar(tmp4[:, :], psu[:, :],
                                            b2sb[:, e * 8 + o:e * 8 + o + 1], None, ALU.add)
                            v.tensor_mul(d, tmp4[:, :], cwbf[:, j * 512:(j + 1) * 512])
                        else:
                            v.tensor_mul(d, psu[:, :], cwbf[:, j * 512:(j + 1) * 512])
                    for o in range(C8):
                        g0 = (gct[0] % 2) * 4
                        gct[0] += 1
                        par = nacc[0] % 2
                        for j in range(NSP2):
                            K.wait("t", se8[g0 + j])
                        def mmo(t, o=o, g0=g0):
                            for c in range(C8):
                                lh = w2b[c][:, o * 128:(o + 1) * 128]
                                for j in range(NSP2):
                                    t.matmul(pb8[g0 + j][:, :], lh,
                                             hebf[c][:, j * 512:(j + 1) * 512],
                                             start=(c == 0), stop=(c == C8 - 1))
                        K.em("t", mmo)
                        for j in range(NSP2):
                            K.inc("t", sm8[g0 + j])
                        K.wait("v", sC, max(0, (nacc[0] - 1) * 16))
                        for j in range(NSP2):
                            K.wait("v", sm8[g0 + j])
                            def ep(v, o=o, j=j, g0=g0, par=par):
                                ep_o2(v, o, j, pb8[g0 + j], par)
                            K.em("v", ep)
                            K.inc("v", se8[g0 + j])
                        for j in range(8):
                            K.wait("g", se8[j], se8[j].n)
                        K.wait("g", sC)
                        first = (e == 0 and h == 0)
                        def acc(g, o=o, par=par, first=first):
                            g.dma_start(out=s_macc[o * 128:(o + 1) * 128, :],
                                        in_=tsb[par][:, :],
                                        **({} if first else {"accum_op": ALU.add})) \
                             .then_inc(sC.h, 16)
                        K.em("g", acc)
                        sC.n += 16
                        nacc[0] += 1
                    K.flush()
            K.barrier("moe_done")

        if DBG_TAPS:
            tapmap = {"x2": s_x2, "x3": s_x3, "xa": s_xa, "macc": s_macc,
                      "kin": cc_k_in, "kout": cc_k_out, "vin": ccvi,
                      "vout": ccvo, "bno": cc_bn_out, "ot": s_ot,
                      "wm": s_wm, "mt": s_wm,
                      "wqin": s_wqin, "mod": s_mod, "dbgq": s_dbgq}
            s_dbg = K.sem("md_st")
            if "vecs" in DBG_TAPS:
                ov = nc.dram_tensor("dbg_vecs", [128, 144], F32, kind="ExternalOutput").ap()
                def dbgv(g, ov=ov):
                    g.dma_start(out=ov[:, :], in_=vecs[:, :]).then_inc(s_dbg.h, 16)
                K.em("g", dbgv)
                s_dbg.n += 16
            for nm in DBG_TAPS:
                if nm == "vecs":
                    continue
                t = tapmap[nm]
                o = nc.dram_tensor("dbg_" + nm, list(t.tensor.shape), t.tensor.dtype,
                                   kind="ExternalOutput").ap()
                def dbgcp(g, t=t, o=o):
                    g.dma_start(out=o[:, :], in_=t[:, :]).then_inc(s_dbg.h, 16)
                K.em("g", dbgcp)
                s_dbg.n += 16
            K.wait("g", s_dbg)
        with contextlib.ExitStack() as ps:
            x3sp = [ps.enter_context(nc.sbuf_tensor(f"f_x{c}", [128, 512], F32))
                    for c in range(C8)]
            msp = [ps.enter_context(nc.sbuf_tensor(f"f_m{c}", [128, 512], F32))
                   for c in range(C8)]
            xtok = ps.enter_context(nc.sbuf_tensor("f_tk", [128, 1024], F32))
            pft = ps.enter_context(nc.psum_tensor("f_pt", [128, 1024], F32))
            f1 = K.sem("f_1"); f2 = K.sem("f_2"); f3 = K.sem("f_3"); f4 = K.sem("f_4")
            fout = K.sem("f_out")
            for sp in range(NSP2):
                c0 = sp * 512
                if sp > 0:
                    K.wait("y", f2, sp)
                def ldf(y, c0=c0):
                    for c in range(C8):
                        y.dma_start(out=x3sp[c][:, :],
                                    in_=s_x3[c * 128:(c + 1) * 128, c0:c0 + 512]) \
                         .then_inc(f1.h, 16)
                        y.dma_start(out=msp[c][:, :],
                                    in_=s_macc[c * 128:(c + 1) * 128, c0:c0 + 512]) \
                         .then_inc(f1.h, 16)
                K.em("y", ldf)
                f1.n += 32 * C8
                K.wait("v", f1)
                def fin(v, c0=c0):
                    for c in range(C8):
                        v.tensor_add(msp[c][:, :], msp[c][:, :], h4res[c][:, c0:c0 + 512])
                        v.scalar_tensor_tensor(x3sp[c][:, :], msp[c][:, :], mc("gb1", c),
                                               x3sp[c][:, :], ALU.mult, ALU.add)
                K.em("v", fin)
                for tt in range(4):
                    K.inc("v", f3)
                    K.wait("t", f3)
                    K.wait("t", f4, f4.n)
                    def ftr(t, tt=tt):
                        for c in range(C8):
                            t.transpose(pft[:, c * 128:(c + 1) * 128],
                                        x3sp[c][:, tt * 128:(tt + 1) * 128], idf[:])
                    K.em("t", ftr)
                    K.inc("t", f3)
                    K.wait("v", f3)
                    def fcp(v):
                        v.tensor_copy(xtok[:, :], pft[:, :])
                    K.em("v", fcp)
                    K.inc("v", f4)
                    K.wait("y", f4)
                    def fst(y, sp=sp, tt=tt):
                        y.dma_start(out=out[sp * 512 + tt * 128:sp * 512 + (tt + 1) * 128, :],
                                    in_=xtok[:, :]).then_inc(fout.h, 16)
                    K.em("y", fst)
                    fout.n += 16
                    K.wait("v", fout)
                K.inc("y", f2)
                K.flush()
            for e in KB.E:
                K.wait(e, fout)
            K.flush()

    gstack.close()
    return nc


def _host_stage(inputs, S):
    TPC = S // TILE // NCORE
    x = np.ascontiguousarray(inputs["L0_attn_x"].reshape(S, D).astype(np.float32))
    shared = {}
    shared["w_in_T"] = np.ascontiguousarray(inputs["L0_attn_w_in"].T)
    shared["w_x_T"] = np.ascontiguousarray(inputs["L0_attn_w_x"].T)
    shared["w_out_T"] = np.ascontiguousarray(inputs["L0_attn_w_out"].T)
    shared["wr_T"] = np.ascontiguousarray(inputs["L0_ffn_W_r"].T)
    shared["wk_T"] = np.ascontiguousarray(inputs["L0_ffn_W_k"].T)
    shared["wv_T"] = np.ascontiguousarray(inputs["L0_ffn_W_v"].T)
    shared["wattn_T"] = np.ascontiguousarray(inputs["L1_attn_W_attn"].T)
    shared["wproj_nat"] = np.ascontiguousarray(inputs["L1_attn_W_proj"])
    shared["wproj_T"] = np.ascontiguousarray(inputs["L1_attn_W_proj"].T)
    shared["la_aT"] = np.ascontiguousarray(inputs["L1_attn_lora_attn_A"].T)
    shared["la_b"] = np.ascontiguousarray(inputs["L1_attn_lora_attn_B"])
    shared["lp_aT"] = np.ascontiguousarray(inputs["L1_attn_lora_proj_A"].T)
    shared["lp_b"] = np.ascontiguousarray(inputs["L1_attn_lora_proj_B"])
    shared["wmod0"] = np.ascontiguousarray(inputs["L0_wmod"])
    shared["wmod1"] = np.ascontiguousarray(inputs["L1_wmod"])
    shared["c01T"] = np.ascontiguousarray(
        np.concatenate([inputs["L0_c"].reshape(-1), inputs["L1_c"].reshape(-1)])[:, None])
    shared["gwi"] = np.ascontiguousarray(inputs["L1_ffn_gate_W"])
    shared["w1e"] = np.ascontiguousarray(inputs["L1_ffn_W1"])
    shared["w2e"] = np.ascontiguousarray(inputs["L1_ffn_W2"])
    shared["b1c"] = np.ascontiguousarray(
        inputs["L1_ffn_b1"].reshape(NEXP, C16, 128).transpose(2, 0, 1).reshape(128, NEXP * C16))
    shared["b2n"] = np.ascontiguousarray(inputs["L1_ffn_b2"])
    shared["b2c"] = np.ascontiguousarray(
        inputs["L1_ffn_b2"].reshape(NEXP, C8, 128).transpose(2, 0, 1).reshape(128, NEXP * C8))
    cols = []
    cols.append(inputs["L0_attn_b_in"][:DI].reshape(C16, 128).T)
    cols.append(inputs["L0_attn_b_conv"].reshape(C16, 128).T)
    cols.append(inputs["L0_attn_b_x"][:DI].reshape(C16, 128).T)
    cols.append(inputs["L0_attn_b_out"].reshape(C8, 128).T)
    cols.append(inputs["L0_ffn_mu_r"].reshape(C8, 128).T)
    cols.append(inputs["L0_ffn_mu_k"].reshape(C8, 128).T)
    cols.append(inputs["L1_attn_bn_gamma"].reshape(24, 128).T)
    cols.append(inputs["L1_attn_bn_beta"].reshape(24, 128).T)
    vt = np.concatenate(cols, axis=1).astype(np.float32)
    assert vt.shape == (128, 120)
    shared["vtab"] = np.ascontiguousarray(vt)
    shared["wconv_c"] = np.ascontiguousarray(
        inputs["L0_attn_w_conv"][:, 0, :].reshape(C16, 128, 4).transpose(1, 0, 2)
        .reshape(128, C16 * 4))
    shared["id_bf"] = np.eye(128, dtype=np.float32).astype(ml_dtypes.bfloat16)
    shared["id_f32"] = np.eye(128, dtype=np.float32)
    shared["rowidx8"] = np.repeat(np.arange(NEXP, dtype=np.float32)[:, None], 128, axis=1)
    maps = []
    for i in range(NCORE):
        m = dict(shared)
        rows = []
        for mslot in range(TPC):
            g = mslot * NCORE + i
            lo = g * TILE - HALO
            if lo < 0:
                rows.append(np.zeros((HALO, D), np.float32))
                rows.append(x[0:g * TILE + TILE])
            else:
                rows.append(x[lo:g * TILE + TILE])
        xs = np.concatenate(rows, axis=0)
        m["xsh"] = np.ascontiguousarray(xs.T)
        f = np.arange(1024)[None, :]
        pp = np.arange(128)[:, None]
        m["trimask"] = np.where(f <= pp + 128 * i, 0.0, NEG).astype(np.float32)
        m["haloflag"] = np.full((128, 1), 0.0 if i == 0 else 1.0, np.float32)
        maps.append(m)
    return maps


def kernel(**inputs):
    S = int(np.prod(inputs["L0_attn_x"].shape[:-1]))
    maps = _host_stage(inputs, S)
    nc = build_nc(S)
    res = run_bass_kernel_spmd(nc, maps, core_ids=list(range(NCORE)))
    TPC = S // TILE // NCORE
    full = np.empty((S, D), np.float32)
    for i in range(NCORE):
        shard = np.asarray(res.results[i]["out"])
        for mslot in range(TPC):
            g = mslot * NCORE + i
            full[g * TILE:(g + 1) * TILE] = shard[mslot * TILE:(mslot + 1) * TILE]
    return full.reshape(inputs["L0_attn_x"].shape)
